# version 06
# baseline (speedup 1.0000x reference)
import numpy as np
import concourse.bass as bass
import concourse.mybir as mybir
from concourse.bass_utils import run_bass_kernel_spmd

F32 = mybir.dt.float32
BF16 = mybir.dt.bfloat16
AF = mybir.ActivationFunctionType
ALU = mybir.AluOpType
AX = mybir.AxisListType

P = 128
T = 2048
WG = T + 3
NFT = 8
NHT = 12
TC = 512
GC = T // TC
GP = 512
NGP = T // GP
FC = 1024
NFC = T // FC
WIN = 512
NW = T // WIN
MT1P = 22
KT2 = 22
NEG = -3.0e38
EPS = 1e-5
NDMASLOT = 8
RG = [[0, 1], [2, 3], [4, 5], [6, 7]]



def _t(a):
    return np.ascontiguousarray(np.asarray(a, np.float32).T)


def _cvec(v, n_tiles):
    return np.ascontiguousarray(np.asarray(v, np.float32).reshape(n_tiles, P).T)


def _rot_tables(t0):
    DH = 64
    n = 4096
    half = np.arange(0, DH, 2, dtype=np.float64)
    inv_freq = 1.0 / 10000.0 ** (half / DH)
    t = np.arange(t0, t0 + T, dtype=np.float64)
    fr = t[None, :] * inv_freq[:, None]
    base = (half + 0.4 * DH) / (1.4 * DH)
    power = (t - n // 2) / float(WIN // 2)
    sc = base[:, None] ** power[None, :]
    cos, sin = np.cos(fr), np.sin(fr)
    QKS = 8.0
    Cq = np.concatenate([cos * sc, cos * sc], 0) * QKS
    Sq = np.concatenate([-sin * sc, sin * sc], 0) * QKS
    Ck = np.concatenate([cos / sc, cos / sc], 0)
    Sk = np.concatenate([-sin / sc, sin / sc], 0)
    return [np.ascontiguousarray(np.tile(a, (2, 1)).astype(np.float32))
            for a in (Cq, Sq, Ck, Sk)]


def _mask_tiles(seq_start):
    m = np.zeros((4, P, 640), np.float32)
    r = np.arange(P)[:, None]
    cc = np.arange(P)[None, :]
    for qc in range(4):
        for b in range(5):
            imj = 512 - 128 * b + r - cc
            bad = (imj < 0) | (imj > 512)
            if seq_start:
                bad = bad | ((qc * 128 - 512 + 128 * b + cc + 0 * r) < 0)
            m[qc][:, b * 128:(b + 1) * 128][bad] = NEG
    return m


def _prep_inputs(inputs):
    f32 = np.float32
    x = np.asarray(inputs["L0_attn_x"], f32)
    sh = {}
    sh["winT"] = _t(inputs["L0_attn_W_in"])
    sh["wgateT"] = _t(inputs["L0_attn_W_gate"])
    sh["rginT"] = _t(inputs["L0_attn_rg_in_w"])
    sh["rggateT"] = _t(inputs["L0_attn_rg_gate_w"])
    sh["woutT"] = _t(inputs["L0_attn_W_out"])
    sh["mgateT"] = _t(inputs["L0_attn_mlp_gate_w"])
    sh["mupT"] = _t(inputs["L0_attn_mlp_up_w"])
    sh["mdownT"] = _t(inputs["L0_attn_mlp_down_w"])
    cw = np.asarray(inputs["L0_attn_conv_w"], f32)[:, 0, :]
    sh["convw"] = np.ascontiguousarray(cw.reshape(NHT, P, 4).transpose(1, 0, 2).reshape(P, NHT * 4))
    sh["convb"] = _cvec(inputs["L0_attn_conv_b"], NHT)
    sh["rgib"] = _cvec(inputs["L0_attn_rg_in_b"], NHT)
    sh["rggb"] = _cvec(inputs["L0_attn_rg_gate_b"], NHT)
    sh["flam"] = _cvec(inputs["L0_attn_forget_lambda"], NHT)
    sh["n1w"] = _cvec(inputs["L0_attn_norm1_w"], NFT)
    sh["n2w"] = _cvec(inputs["L0_attn_norm2_w"], NFT)
    for l in range(2):
        sh[f"wqkv{l}"] = np.ascontiguousarray(np.asarray(inputs["L0_ffn_wqkv"], f32)[l])
        sh[f"wo{l}"] = np.ascontiguousarray(np.asarray(inputs["L0_ffn_wo"], f32)[l])
        w1 = np.asarray(inputs["L0_ffn_w1"], f32)[l]
        b1 = np.asarray(inputs["L0_ffn_b1"], f32)[l]
        a = np.concatenate([w1[:, :2730], np.zeros((1024, 86), f32)], 1).reshape(1024, 22, 128)
        g = np.concatenate([w1[:, 2730:], np.zeros((1024, 86), f32)], 1).reshape(1024, 22, 128)
        sh[f"w1p{l}"] = np.ascontiguousarray(np.stack([a, g], 2).reshape(1024, 44 * 128))
        ba = np.concatenate([b1[:2730], np.zeros(86, f32)]).reshape(22, 128)
        bg = np.concatenate([b1[2730:], np.zeros(86, f32)]).reshape(22, 128)
        sh[f"b1p{l}"] = np.ascontiguousarray(np.stack([ba, bg], 1).reshape(44, 128).T)
        w2 = np.asarray(inputs["L0_ffn_w2"], f32)[l]
        sh[f"w2p{l}"] = np.ascontiguousarray(np.concatenate([w2, np.zeros((86, 1024), f32)], 0))
        sh[f"b2{l}"] = _cvec(np.asarray(inputs["L0_ffn_b2"], f32)[l], NFT)
        sh[f"lng{l}"] = _cvec(np.asarray(inputs["L0_ffn_ln_g"], f32)[l], NFT)
        sh[f"lnb{l}"] = _cvec(np.asarray(inputs["L0_ffn_ln_b"], f32)[l], NFT)
        qs = np.asarray(inputs["L0_ffn_q_scale"], f32)[l]
        ks = np.asarray(inputs["L0_ffn_k_scale"], f32)[l]
        idx = np.arange(P) % 64
        swi = (idx + 32) % 64
        sh[f"qs{l}"] = np.ascontiguousarray(qs[idx][:, None])
        sh[f"qssw{l}"] = np.ascontiguousarray(qs[swi][:, None])
        sh[f"ks{l}"] = np.ascontiguousarray(ks[idx][:, None])
        sh[f"kssw{l}"] = np.ascontiguousarray(ks[swi][:, None])
    sh["ident"] = np.eye(P, dtype=f32)
    pm = np.zeros((P, P), f32)
    for p in range(P):
        pm[(p % 64 + 32) % 64 + 64 * (p // 64), p] = 1.0
    sh["perm"] = pm
    sh["onescol"] = np.ones((1, P), f32)
    sh["onesk"] = np.ones((P, 1), f32)
    sh["epsc"] = np.full((P, 1), 1e-5, f32)
    pat2 = np.zeros((2, P), f32)
    pat2[0, :64] = 1.0
    pat2[1, 64:] = 1.0
    sh["pat2"] = pat2
    bo = np.zeros((P, 2), f32)
    bo[:64, 0] = 1.0
    bo[64:, 1] = 1.0
    sh["blockones"] = bo
    sh["maskB"] = _mask_tiles(False)
    mask_a = _mask_tiles(True)
    per_core = []
    for c in range(8):
        half = c % 2
        b = c // 2
        t0 = half * T
        d = {}
        xc = np.zeros((1024, WG), f32)
        xc[:, 3:] = x[b, t0:t0 + T, :].T
        if half == 1:
            xc[:, :3] = x[b, t0 - 3:t0, :].T
        d["x"] = np.ascontiguousarray(xc)
        d["selmask"] = np.full((P, 1), 1.0 if half == 1 else 0.0, f32)
        d["tabCq"], d["tabSq"], d["tabCk"], d["tabSk"] = _rot_tables(t0)
        d["maskA"] = mask_a if half == 0 else sh["maskB"]
        per_core.append(d)
    return sh, per_core



ENGINES = ("sync", "scalar", "vector", "tensor", "gpsimd")


class Buf:
    __slots__ = ("name", "w", "r")

    def __init__(self, name):
        self.name = name
        self.w = None
        self.r = []


def ts(i, n=P):
    return slice(i * n, (i + 1) * n)


def npieces(width, maxn=512):
    out, o = [], 0
    while o < width:
        w = min(maxn, width - o)
        out.append((o, w))
        o += w
    return out


class Prog:
    def __init__(self, taps=(), upto=None):
        self.upto = upto
        self.nc = bass.Bass("TRN2", target_bir_lowering=False)
        self.stream = {e: [] for e in ENGINES}
        self.sems = {}
        self.cnt = {}
        self.waited = {e: {} for e in ENGINES}
        self._ctx = []
        self.dmaidx = {}
        self.DT = {}
        self.SB = {}
        self.PS = {}
        self.BF = {}
        self.taps = set(taps)
        self.nb = 0

    def ctx(self, cm):
        v = cm.__enter__()
        self._ctx.append(cm)
        return v

    def scope(self):
        return len(self._ctx)

    def close_scope(self, mark):
        while len(self._ctx) > mark:
            self._ctx.pop().__exit__(None, None, None)

    def buf(self, name=None):
        self.nb += 1
        return Buf(name or f"b{self.nb}")

    def sem(self, name):
        if name not in self.sems:
            self.sems[name] = self.ctx(self.nc.semaphore(name))
            self.cnt[name] = 0
        return self.sems[name]

    def op(self, eng, fn, reads=(), writes=(), dma=False):
        extra_wait = None
        if dma:
            idx = self.dmaidx.get(eng, 0)
            self.dmaidx[eng] = idx + 1
            slot = idx % NDMASLOT
            semname, amt, tag = f"dm_{eng}{slot}", 16, "dma"
            self.sem(semname)
            prev = self.cnt[semname]
            if prev > 0:
                extra_wait = (semname, prev)
        else:
            semname, amt, tag = "tk_" + eng, 1, eng
        self.sem(semname)
        want = {}

        def add(tok):
            if tok is None:
                return
            ttag, tsem, tval = tok
            if want.get(tsem, 0) < tval:
                want[tsem] = tval
        for b in reads:
            add(b.w)
        for b in writes:
            add(b.w)
            for t in b.r:
                add(t)
        if extra_wait is not None:
            s, v = extra_wait
            if want.get(s, 0) < v:
                want[s] = v
        wd = self.waited[eng]
        waits = []
        for s, v in want.items():
            if wd.get(s, 0) < v:
                waits.append((s, v))
                wd[s] = v
        self.cnt[semname] += amt
        val = self.cnt[semname]
        sems = self.sems

        def run(e, fn=fn, waits=waits, semname=semname, amt=amt):
            for s, v in waits:
                e.wait_ge(sems[s], v)
            inst = fn(e)
            assert inst is not None, f"op on {semname} returned None"
            inst.then_inc(sems[semname], amt)

        self.stream[eng].append(run)
        tok = (tag, semname, val)
        for b in reads:
            b.r.append(tok)
        for b in writes:
            b.w = tok
            b.r = []
        return tok

    def raw(self, eng, fn):
        self.stream[eng].append(fn)

    def barrier(self):
        targets = [(s, v) for s, v in self.cnt.items() if v > 0]
        for e in ENGINES:
            wd = self.waited[e]
            todo = [(s, v) for (s, v) in targets if wd.get(s, 0) < v]
            for s, v in todo:
                wd[s] = v
            sems = self.sems

            def wt(x, todo=todo):
                for s, v in todo:
                    x.wait_ge(sems[s], v)
            self.raw(e, wt)

    def din(self, name, shape, dtype=F32):
        self.DT[name] = self.nc.dram_tensor(name, list(shape), dtype, kind="ExternalInput").ap()

    def dout(self, name, shape, dtype=F32):
        self.DT[name] = self.nc.dram_tensor(name, list(shape), dtype, kind="ExternalOutput").ap()

    def dint(self, name, shape, dtype=F32):
        self.DT[name] = self.nc.dram_tensor(name, list(shape), dtype, kind="Internal").ap()

    def sb(self, name, shape, dtype=F32, nbufs=1):
        self.nb += 1
        self.SB[name] = self.ctx(self.nc.sbuf_tensor(f"sb{self.nb}_{name}", list(shape), dtype))
        if nbufs == 1:
            self.BF[name] = self.buf(name)
            return self.BF[name]
        bl = [self.buf(f"{name}:{i}") for i in range(nbufs)]
        self.BF[name] = bl
        return bl

    def psum(self, name, shape, dtype=F32):
        self.PS[name] = self.ctx(self.nc.psum_tensor("ps_" + name, list(shape), dtype))
        self.BF[name] = self.buf(name)

    def mmg(self, psname, emits, reads):
        def fn(e):
            inst = None
            for em in emits:
                inst = em(e)
            return inst
        return self.op("tensor", fn, reads=reads, writes=[self.BF[psname]])

    def dma(self, eng, out_ap, in_ap, reads=(), writes=()):
        return self.op(eng, lambda e: e.dma_start(out=out_ap, in_=in_ap),
                       reads=reads, writes=writes, dma=True)

    def tap(self, name, ap, rbufs):
        if name in self.taps:
            self.dma("sync", self.DT["tap_" + name], ap, reads=rbufs)

    def bcast(self, psname, row_ap, row_buf, width):
        SB, PS = self.SB, self.PS
        scrB = self.BF["bc_t"]
        self.op("vector", lambda e: e.tensor_copy(
            SB["bc_hi"][:1, 0:width], row_ap), reads=[row_buf], writes=[scrB])
        self.op("vector", lambda e: e.tensor_sub(
            SB["bc_t"][:1, 0:width], row_ap, SB["bc_hi"][:1, 0:width]),
            reads=[row_buf, scrB], writes=[scrB])
        self.op("vector", lambda e: e.tensor_copy(
            SB["bc_lo"][:1, 0:width], SB["bc_t"][:1, 0:width]), reads=[scrB], writes=[scrB])
        emits = []
        for o, w in npieces(width):
            emits.append(lambda e, o=o, w=w: e.matmul(
                PS[psname][:, o:o + w], SB["onescol"][:1, :], SB["bc_hi"][:1, o:o + w],
                start=True, stop=False))
            emits.append(lambda e, o=o, w=w: e.matmul(
                PS[psname][:, o:o + w], SB["onescol"][:1, :], SB["bc_lo"][:1, o:o + w],
                start=False, stop=True))
        return self.mmg(psname, emits, [scrB, self.BF["onescol"]])

    def build(self):
        nc, DT, SB, PS, BF = self.nc, self.DT, self.SB, self.PS, self.BF
        self.din("x", (1024, WG))
        self.din("selmask", (P, 1))
        for n in ("winT", "wgateT", "mgateT", "mupT"):
            self.din(n, (1024, 1536))
        for n in ("rginT", "rggateT"):
            self.din(n, (1536, 1536))
        for n in ("woutT", "mdownT"):
            self.din(n, (1536, 1024))
        self.din("convw", (P, NHT * 4)); self.din("convb", (P, NHT))
        self.din("rgib", (P, NHT)); self.din("rggb", (P, NHT)); self.din("flam", (P, NHT))
        self.din("n1w", (P, NFT)); self.din("n2w", (P, NFT))
        for l in range(2):
            self.din(f"wqkv{l}", (1024, 1536)); self.din(f"wo{l}", (512, 1024))
            self.din(f"w1p{l}", (1024, 5632)); self.din(f"b1p{l}", (P, 44))
            self.din(f"w2p{l}", (2816, 1024)); self.din(f"b2{l}", (P, NFT))
            self.din(f"lng{l}", (P, NFT)); self.din(f"lnb{l}", (P, NFT))
            for n in ("qs", "qssw", "ks", "kssw"):
                self.din(f"{n}{l}", (P, 1))
        self.din("ident", (P, P)); self.din("perm", (P, P))
        self.din("onescol", (1, P)); self.din("onesk", (P, 1)); self.din("epsc", (P, 1))
        self.din("pat2", (2, P)); self.din("blockones", (P, 2))
        self.din("maskA", (4, P, 640)); self.din("maskB", (4, P, 640))
        for n in ("tabCq", "tabSq", "tabCk", "tabSk"):
            self.din(n, (P, T))
        self.dout("y", (1024, T))
        self.dint("x1scr", (1024, T))
        self.dint("gascr", (NHT * P, T), BF16)
        self.dint("cin_carry", (P, NHT)); self.dint("cout_carry", (2, P, NHT))
        for l in range(2):
            self.dint(f"cin_kv{l}", (8, P, 512), BF16)
            self.dint(f"cout_kv{l}", (2, 8, P, 512), BF16)
            self.dint(f"cin_hc{l}", (NFT, P)); self.dint(f"cout_hc{l}", (2, NFT, P))
            self.dint(f"cin_pa{l}", (NFT, P)); self.dint(f"cout_pa{l}", (2, NFT, P))
        tapshapes = {
            "g_gb": ((P, NFT, TC + 3), BF16), "g_uc": ((P, NHT, TC), F32),
            "g_hlast": ((P, NHT), F32), "g_prod": ((P, NHT, T), BF16),
            "g_A": ((P, NFT, GP), F32), "x1": ((P, NFT, GP), F32),
            "f_h1": ((P, NFT, 1 + T), F32),
            "f_q": ((P, 4, T), BF16), "f_k": ((P, 4, 2560), BF16),
            "f_v": ((P, 20, 512), BF16), "f_o": ((P, 4, 512), BF16),
            "f_pa": ((P, NFT, 1 + T), F32), "f_out": ((P, NFT, 1 + T), F32),
        }
        for tname in self.taps:
            shp, dt = tapshapes[tname]
            self.dout("tap_" + tname, shp, dt)

        for n in ("pa", "pb", "pc"):
            self.psum(n, (P, 512))
        self.psum("ps0", (P, 640)); self.psum("ps1", (P, 640))
        self.psum("pt", (P, 512), BF16)
        self.BF["pt0"] = self.buf("pt0"); self.BF["pt1"] = self.buf("pt1")
        self.BF["pt2"] = self.buf("pt2"); self.BF["pt3"] = self.buf("pt3")

        for n, shp, dt in (("ident", (P, P), BF16), ("perm", (P, P), BF16),
                           ("onescol", (1, P), BF16), ("onesk", (P, 1), BF16),
                           ("pat2", (2, P), BF16), ("blockones", (P, 2), BF16),
                           ("selmask", (P, 1), F32), ("convw", (P, NHT * 4), F32),
                           ("convb", (P, NHT), F32), ("rgib", (P, NHT), F32),
                           ("rggb", (P, NHT), F32), ("negsp8", (P, NHT), F32),
                           ("n1w", (P, NFT), F32), ("n2w", (P, NFT), F32),
                           ("flam", (P, NHT), F32), ("spt", (P, NHT), F32),
                           ("hlast", (P, NHT), F32), ("alast", (P, NHT), F32),
                           ("carry", (P, NHT), F32), ("epsc", (P, 1), F32),
                           ("bc_t", (1, 640), F32), ("bc_hi", (1, 640), BF16),
                           ("bc_lo", (1, 640), BF16)):
            self.sb(n, shp, dt)
        smalls2 = []
        for l in range(2):
            for n in (f"qs{l}", f"qssw{l}", f"ks{l}", f"kssw{l}"):
                self.sb(n, (P, 1), F32)
                smalls2.append(n)
            for n in (f"b1p{l}",):
                self.sb(n, (P, 44), F32)
                smalls2.append(n)
            for n in (f"b2{l}", f"lng{l}", f"lnb{l}"):
                self.sb(n, (P, NFT), F32)
                smalls2.append(n)

        for n in ("ident", "perm", "onescol", "onesk", "pat2", "blockones"):
            self.dma("gpsimd", SB[n][:, :] if n != "onescol" else SB[n][:1, :],
                     DT[n][:, :] if n != "onescol" else DT[n][:1, :], writes=[BF[n]])
        for n in ("selmask", "convw", "convb", "rgib", "rggb", "n1w", "n2w", "flam", "epsc"):
            self.dma("sync", SB[n][:, :], DT[n][:, :], writes=[BF[n]])
        for n in smalls2:
            self.dma("sync", SB[n][:, :], DT[n][:, :], writes=[BF[n]])
        ops = self.op
        ops("scalar", lambda e: e.activation(out=SB["spt"][:, :], in_=SB["flam"][:, :],
                                             func=AF.Abs), reads=[BF["flam"]], writes=[BF["spt"]])
        ops("scalar", lambda e: e.activation(out=SB["spt"][:, :], in_=SB["spt"][:, :],
                                             func=AF.Exp, scale=-1.0), reads=[BF["spt"]], writes=[BF["spt"]])
        ops("scalar", lambda e: e.activation(out=SB["spt"][:, :], in_=SB["spt"][:, :],
                                             func=AF.Ln, bias=1.0), reads=[BF["spt"]], writes=[BF["spt"]])
        ops("vector", lambda e: e.tensor_scalar_max(SB["negsp8"][:, :], SB["flam"][:, :], 0.0),
            reads=[BF["flam"]], writes=[BF["negsp8"]])
        ops("vector", lambda e: e.tensor_add(SB["negsp8"][:, :], SB["negsp8"][:, :], SB["spt"][:, :]),
            reads=[BF["negsp8"], BF["spt"]], writes=[BF["negsp8"]])
        ops("vector", lambda e: e.tensor_scalar_mul(SB["negsp8"][:, :], SB["negsp8"][:, :], -8.0),
            reads=[BF["negsp8"]], writes=[BF["negsp8"]])
        ops("vector", lambda e: e.memset(SB["hlast"][:, :], 0.0), writes=[BF["hlast"]])
        ops("vector", lambda e: e.memset(SB["alast"][:, :], 1.0), writes=[BF["alast"]])

        if self.upto != "consts":
            self.phaseG()
            if self.upto is None:
                self.phaseF()

        self.barrier()
        blk = self.ctx(nc.Block())
        for eng in ENGINES:
            def body(e, eng=eng):
                for fn in self.stream[eng]:
                    fn(e)
            getattr(blk, eng)(body)
        self.close_scope(0)
        return nc

    class WStrip:
        def __init__(self, prog, name, nk, mc, nbuf=2):
            self.p = prog
            self.nk, self.mc, self.nbuf = nk, mc, nbuf
            prog.sb(name, (P, nbuf, nk, mc), BF16)
            self.t = prog.SB[name]
            self.bufs = [prog.buf(f"{name}:{i}") for i in range(nbuf)]
            self.i = 0
            prog.op("vector", lambda e: e.memset(self.t[:, :, :, :], 0.0),
                    writes=self.bufs)

        def load(self, dram, c0, nk=None, mc=None):
            nk = nk or self.nk
            mc = mc or self.mc
            sl = self.i % self.nbuf
            self.p.dma("gpsimd", self.t[:, sl, 0:nk, 0:mc],
                       self.p.DT[dram][0:nk * P, c0:c0 + mc].rearrange("(k p) m -> p k m", p=P),
                       writes=[self.bufs[sl]])
            self.i += 1
            return sl

        def ap(self, sl, k, mcs=None):
            return self.t[:, sl, k, mcs] if mcs is not None else self.t[:, sl, k, :]

        def b(self, sl):
            return self.bufs[sl]

    def phaseG(self):
        SB, PS, BF, DT = self.SB, self.PS, self.BF, self.DT
        op, mmg, dma, bcast = self.op, self.mmg, self.dma, self.bcast
        gall = self.scope()
        self.sb("mu_row", (1, WG)); self.sb("r_row", (1, WG))
        rowsB = BF["mu_row"]
        BF["r_row"] = rowsB
        prodB = self.sb("g_prod", (P, NHT, T), BF16, nbufs=NHT)

        g1 = self.scope()
        ucbB = self.sb("g_ucb", (P, NHT, TC), BF16, nbufs=NHT)
        uc32B = self.sb("g_uc32", (P, NHT, TC), F32, nbufs=NHT)
        gbufB = self.sb("g_gbuf", (P, NFT, TC + 3), BF16, nbufs=NFT)
        xtB = self.sb("g_xt", (P, NFT, TC + 3), F32, nbufs=NFT)
        xbB = self.sb("g_xb", (P, 2, TC + 3), BF16, nbufs=2)
        x2bB = self.sb("g_x2b", (P, 2, TC + 3), BF16, nbufs=2)
        for n in ("g_wk1", "g_wk2", "g_wk3", "g_wk4", "g_wk5"):
            self.sb(n, (P, 2, TC), F32, nbufs=2)
        self.sb("g_gax", (P, 2, TC), BF16, nbufs=2)
        wk1B, wk2B, wk3B, wk4B, wk5B = (BF["g_wk1"], BF["g_wk2"], BF["g_wk3"],
                                        BF["g_wk4"], BF["g_wk5"])
        gaxB = BF["g_gax"]
        self.sb("g_rba", (1, TC + 3), BF16); self.sb("g_rbb", (1, TC + 3), BF16)
        self.sb("g_rowa", (1, TC + 3)); self.sb("g_rowb", (1, TC + 3)); self.sb("g_rowc", (1, TC + 3))
        growB = BF["g_rowa"]
        for n in ("g_rowb", "g_rowc", "g_rba", "g_rbb"):
            BF[n] = growB
        stWin = Prog.WStrip(self, "sWin", 8, P)
        stWgt = Prog.WStrip(self, "sWgt", 8, P)
        stRgi = Prog.WStrip(self, "sRgi", NHT, P)
        stRgg = Prog.WStrip(self, "sRgg", NHT, P)

        for c in range(GC):
            xc0 = c * TC
            cw = TC + 3
            for ft in range(NFT):
                dma("sync", SB["g_xt"][:, ft, :], DT["x"][ts(ft), xc0:xc0 + cw],
                    writes=[xtB[ft]])
            for ft in range(NFT):
                sl = ft % 2
                op("vector", lambda e, ft=ft, sl=sl: e.tensor_copy(
                    SB["g_xb"][:, sl, :], SB["g_xt"][:, ft, :]),
                    reads=[xtB[ft]], writes=[xbB[sl]])
                op("scalar", lambda e, ft=ft, sl=sl: e.activation(
                    out=SB["g_x2b"][:, sl, :], in_=SB["g_xt"][:, ft, :], func=AF.Square),
                    reads=[xtB[ft]], writes=[x2bB[sl]])
                mmg("ps0", [
                    (lambda e, o=o, w=w, sl=sl, ft=ft: e.matmul(
                        PS["ps0"][0:1, o:o + w], SB["onesk"][:, :1],
                        SB["g_xb"][:, sl, o:o + w], start=ft == 0, stop=ft == NFT - 1))
                    for o, w in npieces(cw)], [xbB[sl], BF["onesk"]])
                mmg("ps1", [
                    (lambda e, o=o, w=w, sl=sl, ft=ft: e.matmul(
                        PS["ps1"][0:1, o:o + w], SB["onesk"][:, :1],
                        SB["g_x2b"][:, sl, o:o + w], start=ft == 0, stop=ft == NFT - 1))
                    for o, w in npieces(cw)], [x2bB[sl], BF["onesk"]])
            op("scalar", lambda e, xc0=xc0, cw=cw: e.activation(
                out=SB["mu_row"][:1, xc0:xc0 + cw], in_=PS["ps0"][0:1, 0:cw],
                func=AF.Copy, scale=1.0 / 1024), reads=[BF["ps0"]], writes=[rowsB])
            op("scalar", lambda e, cw=cw: e.activation(
                out=SB["g_rowb"][:1, 0:cw], in_=PS["ps1"][0:1, 0:cw],
                func=AF.Copy, scale=1.0 / 1024), reads=[BF["ps1"]], writes=[growB])
            op("vector", lambda e, xc0=xc0, cw=cw: e.tensor_tensor(
                SB["g_rowa"][:1, 0:cw], SB["mu_row"][:1, xc0:xc0 + cw],
                SB["mu_row"][:1, xc0:xc0 + cw], op=ALU.mult), reads=[rowsB], writes=[growB])
            op("vector", lambda e, cw=cw: e.tensor_sub(
                SB["g_rowb"][:1, 0:cw], SB["g_rowb"][:1, 0:cw], SB["g_rowa"][:1, 0:cw]),
                reads=[growB], writes=[growB])
            op("scalar", lambda e, cw=cw: e.activation(
                out=SB["g_rowc"][:1, 0:cw], in_=SB["g_rowb"][:1, 0:cw],
                func=AF.Sqrt, bias=SB["epsc"][0:1, 0:1]), reads=[growB, BF["epsc"]], writes=[growB])
            op("vector", lambda e, xc0=xc0, cw=cw: e.reciprocal(
                SB["r_row"][:1, xc0:xc0 + cw], SB["g_rowc"][:1, 0:cw]),
                reads=[growB], writes=[rowsB])
            op("vector", lambda e, xc0=xc0, cw=cw: e.tensor_tensor(
                SB["g_rowa"][:1, 0:cw], SB["g_rowb"][:1, 0:cw],
                SB["r_row"][:1, xc0:xc0 + cw], op=ALU.mult), reads=[growB, rowsB], writes=[growB])
            op("vector", lambda e, xc0=xc0, cw=cw: e.tensor_tensor(
                SB["g_rowa"][:1, 0:cw], SB["g_rowa"][:1, 0:cw],
                SB["r_row"][:1, xc0:xc0 + cw], op=ALU.mult), reads=[growB, rowsB], writes=[growB])
            op("scalar", lambda e, cw=cw: e.activation(
                out=SB["g_rowa"][:1, 0:cw], in_=SB["g_rowa"][:1, 0:cw],
                func=AF.Sqrt, bias=SB["epsc"][0:1, 0:1]), reads=[growB, BF["epsc"]], writes=[growB])
            op("vector", lambda e, cw=cw: e.reciprocal(
                SB["g_rowc"][:1, 0:cw], SB["g_rowa"][:1, 0:cw]),
                reads=[growB], writes=[growB])
            op("vector", lambda e, xc0=xc0, cw=cw: e.tensor_tensor(
                SB["g_rowc"][:1, 0:cw], SB["g_rowc"][:1, 0:cw],
                SB["r_row"][:1, xc0:xc0 + cw], op=ALU.mult), reads=[growB, rowsB], writes=[growB])
            self.bcast("ps0", SB["mu_row"][:1, xc0:xc0 + cw], rowsB, cw)
            self.bcast("ps1", SB["g_rowc"][:1, 0:cw], growB, cw)
            for ft in range(NFT):
                op("vector", lambda e, ft=ft, cw=cw: e.tensor_sub(
                    SB["g_xt"][:, ft, :], SB["g_xt"][:, ft, :], PS["ps0"][:, 0:cw]),
                    reads=[xtB[ft], BF["ps0"]], writes=[xtB[ft]])
                op("vector", lambda e, ft=ft, cw=cw: e.scalar_tensor_tensor(
                    SB["g_gbuf"][:, ft, :], SB["g_xt"][:, ft, :], SB["n1w"][:, ft:ft + 1],
                    PS["ps1"][:, 0:cw], op0=ALU.mult, op1=ALU.mult),
                    reads=[xtB[ft], BF["n1w"], BF["ps1"]], writes=[gbufB[ft]])
            if c == 0:
                self.tap("g_gb", SB["g_gbuf"][:, :, :], gbufB)

            if self.upto == "Gstats":
                continue
            for m in range(NHT):
                sl = stWin.load("winT", m * P)
                psn = "ps0" if m % 2 == 0 else "ps1"
                mmg(psn, [
                    (lambda e, o=o, w=w, k=k, sl=sl, psn=psn: e.matmul(
                        PS[psn][:, o:o + w], stWin.ap(sl, k),
                        SB["g_gbuf"][:, k, o:o + w], start=k == 0, stop=k == 7))
                    for o, w in npieces(cw) for k in range(8)],
                    [stWin.b(sl)] + gbufB)
                mq = m % 2
                op("vector", lambda e, psn=psn, m=m, mq=mq: e.tensor_scalar(
                    SB["g_wk1"][:, mq, :], PS[psn][:, 0:TC], SB["convw"][:, 4 * m:4 * m + 1],
                    None, op0=ALU.mult), reads=[BF[psn], BF["convw"]], writes=[wk1B[mq]])
                op("vector", lambda e, psn=psn, m=m, mq=mq: e.scalar_tensor_tensor(
                    SB["g_wk2"][:, mq, :], PS[psn][:, 1:TC + 1], SB["convw"][:, 4 * m + 1:4 * m + 2],
                    SB["g_wk1"][:, mq, :], op0=ALU.mult, op1=ALU.add),
                    reads=[BF[psn], BF["convw"], wk1B[mq]], writes=[wk2B[mq]])
                op("vector", lambda e, psn=psn, m=m, mq=mq: e.scalar_tensor_tensor(
                    SB["g_wk1"][:, mq, :], PS[psn][:, 2:TC + 2], SB["convw"][:, 4 * m + 2:4 * m + 3],
                    SB["g_wk2"][:, mq, :], op0=ALU.mult, op1=ALU.add),
                    reads=[BF[psn], BF["convw"], wk2B[mq]], writes=[wk1B[mq]])
                op("vector", lambda e, psn=psn, m=m, mq=mq: e.scalar_tensor_tensor(
                    SB["g_wk2"][:, mq, :], PS[psn][:, 3:TC + 3], SB["convw"][:, 4 * m + 3:4 * m + 4],
                    SB["g_wk1"][:, mq, :], op0=ALU.mult, op1=ALU.add),
                    reads=[BF[psn], BF["convw"], wk1B[mq]], writes=[wk2B[mq]])
                op("vector", lambda e, m=m, mq=mq: e.tensor_scalar(
                    SB["g_uc32"][:, m, :], SB["g_wk2"][:, mq, :], SB["convb"][:, m:m + 1],
                    None, op0=ALU.add),
                    reads=[wk2B[mq], BF["convb"]], writes=[uc32B[m]])
                op("vector", lambda e, m=m: e.tensor_copy(
                    SB["g_ucb"][:, m, :], SB["g_uc32"][:, m, :]),
                    reads=[uc32B[m]], writes=[ucbB[m]])
            if c == 0:
                self.tap("g_uc", SB["g_uc32"][:, :, :], uc32B)

            if self.upto == "Gu":
                continue
            def rgA(m):
                mq = m % 2
                igb = "pa" if mq == 0 else "ps0"
                rgb = "pb" if mq == 0 else "ps1"
                sli = stRgi.load("rginT", m * P)
                slg = stRgg.load("rggateT", m * P)
                mmg(igb, [
                    (lambda e, k=k, sli=sli, igb=igb: e.matmul(
                        PS[igb][:, 0:TC], stRgi.ap(sli, k), SB["g_ucb"][:, k, :],
                        start=k == 0, stop=k == NHT - 1))
                    for k in range(NHT)], [stRgi.b(sli)] + ucbB)
                mmg(rgb, [
                    (lambda e, k=k, slg=slg, rgb=rgb: e.matmul(
                        PS[rgb][:, 0:TC], stRgg.ap(slg, k), SB["g_ucb"][:, k, :],
                        start=k == 0, stop=k == NHT - 1))
                    for k in range(NHT)], [stRgg.b(slg)] + ucbB)

            def rgSig(m):
                mq = m % 2
                igb = "pa" if mq == 0 else "ps0"
                rgb = "pb" if mq == 0 else "ps1"
                op("scalar", lambda e, m=m, mq=mq, rgb=rgb: e.activation(
                    out=SB["g_wk1"][:, mq, :], in_=PS[rgb][:, 0:TC], func=AF.Sigmoid,
                    bias=SB["rggb"][:, m:m + 1]), reads=[BF[rgb], BF["rggb"]], writes=[wk1B[mq]])
                op("scalar", lambda e, m=m, mq=mq, igb=igb: e.activation(
                    out=SB["g_wk3"][:, mq, :], in_=PS[igb][:, 0:TC], func=AF.Sigmoid,
                    bias=SB["rgib"][:, m:m + 1]), reads=[BF[igb], BF["rgib"]], writes=[wk3B[mq]])

            def rgExp(m):
                mq = m % 2
                op("scalar", lambda e, m=m, mq=mq: e.activation(
                    out=SB["g_wk2"][:, mq, :], in_=SB["g_wk1"][:, mq, :], func=AF.Exp,
                    scale=SB["negsp8"][:, m:m + 1]),
                    reads=[wk1B[mq], BF["negsp8"]], writes=[wk2B[mq]])

            def rgV1a(m):
                mq = m % 2
                op("vector", lambda e, mq=mq: e.tensor_mul(
                    SB["g_wk4"][:, mq, :], SB["g_wk2"][:, mq, :], SB["g_wk2"][:, mq, :]),
                    reads=[wk2B[mq]], writes=[wk4B[mq]])
                op("vector", lambda e, mq=mq: e.tensor_scalar(
                    SB["g_wk4"][:, mq, :], SB["g_wk4"][:, mq, :], -1.0, 1.0 + 1e-6,
                    op0=ALU.mult, op1=ALU.add), reads=[wk4B[mq]], writes=[wk4B[mq]])

            def rgSq(m):
                mq = m % 2
                op("scalar", lambda e, mq=mq: e.activation(
                    out=SB["g_wk4"][:, mq, :], in_=SB["g_wk4"][:, mq, :], func=AF.Sqrt),
                    reads=[wk4B[mq]], writes=[wk4B[mq]])

            def rgV1b(m):
                mq = m % 2
                op("vector", lambda e, mq=mq: e.tensor_mul(
                    SB["g_wk3"][:, mq, :], SB["g_wk3"][:, mq, :], SB["g_wk4"][:, mq, :]),
                    reads=[wk3B[mq], wk4B[mq]], writes=[wk3B[mq]])
                op("vector", lambda e, m=m, mq=mq: e.tensor_mul(
                    SB["g_wk3"][:, mq, :], SB["g_wk3"][:, mq, :], SB["g_uc32"][:, m, :]),
                    reads=[wk3B[mq], uc32B[m]], writes=[wk3B[mq]])

            def rgScan(m):
                mq = m % 2
                op("vector", lambda e, m=m, mq=mq: e.tensor_tensor_scan(
                    SB["g_wk5"][:, mq, :], SB["g_wk2"][:, mq, :], SB["g_wk3"][:, mq, :],
                    SB["hlast"][:, m:m + 1], op0=ALU.mult, op1=ALU.add),
                    reads=[wk2B[mq], wk3B[mq], BF["hlast"]], writes=[wk5B[mq]])
                op("vector", lambda e, m=m, mq=mq: e.tensor_copy(
                    SB["hlast"][:, m:m + 1], SB["g_wk5"][:, mq, TC - 1:TC]),
                    reads=[wk5B[mq]], writes=[BF["hlast"]])
                op("vector", lambda e, m=m, mq=mq: e.tensor_tensor_scan(
                    SB["g_wk4"][:, mq, :], SB["g_wk2"][:, mq, :], SB["g_wk2"][:, mq, :],
                    SB["alast"][:, m:m + 1], op0=ALU.mult, op1=ALU.bypass),
                    reads=[wk2B[mq], BF["alast"]], writes=[wk4B[mq]])
                op("vector", lambda e, m=m, mq=mq: e.tensor_copy(
                    SB["alast"][:, m:m + 1], SB["g_wk4"][:, mq, TC - 1:TC]),
                    reads=[wk4B[mq]], writes=[BF["alast"]])

            def rgGate(m):
                mq = m % 2
                gtb = "pc" if mq == 0 else "pa"
                slw = stWgt.load("wgateT", m * P)
                mmg(gtb, [
                    (lambda e, k=k, slw=slw, gtb=gtb: e.matmul(
                        PS[gtb][:, 0:TC], stWgt.ap(slw, k), SB["g_gbuf"][:, k, 3:3 + TC],
                        start=k == 0, stop=k == 7))
                    for k in range(8)], [stWgt.b(slw)] + gbufB)

            def rgGel(m):
                mq = m % 2
                gtb = "pc" if mq == 0 else "pa"
                op("scalar", lambda e, mq=mq, gtb=gtb: e.activation(
                    out=SB["g_wk1"][:, mq, :], in_=PS[gtb][:, 0:TC], func=AF.Gelu),
                    reads=[BF[gtb]], writes=[wk1B[mq]])

            def rgProd(m, c=c):
                mq = m % 2
                op("gpsimd", lambda e, m=m, mq=mq, c=c: e.tensor_mul(
                    SB["g_prod"][:, m, c * TC:(c + 1) * TC], SB["g_wk1"][:, mq, :],
                    SB["g_wk5"][:, mq, :]),
                    reads=[wk1B[mq], wk5B[mq]], writes=[prodB[m]])
                op("gpsimd", lambda e, mq=mq: e.tensor_mul(
                    SB["g_gax"][:, mq, :], SB["g_wk1"][:, mq, :], SB["g_wk4"][:, mq, :]),
                    reads=[wk1B[mq], wk4B[mq]], writes=[gaxB[mq]])
                dma("sync", DT["gascr"][ts(m), c * TC:(c + 1) * TC], SB["g_gax"][:, mq, :],
                    reads=[gaxB[mq]])

            for j in range(NHT // 2):
                m0, m1 = 2 * j, 2 * j + 1
                rgA(m0)
                rgA(m1)
                rgSig(m0)
                rgSig(m1)
                rgExp(m0)
                rgExp(m1)
                rgV1a(m0)
                rgV1a(m1)
                rgSq(m0)
                rgSq(m1)
                rgV1b(m0)
                rgV1b(m1)
                rgScan(m0)
                rgScan(m1)
                rgGate(m0)
                rgGate(m1)
                rgGel(m0)
                rgGel(m1)
                rgProd(m0)
                rgProd(m1)

        if self.upto in ("Gstats", "Gu", "Grg"):
            self.barrier()
            self.close_scope(g1)
            self.close_scope(gall)
            return
        ccB = self.buf("cc")
        dma("sync", DT["cin_carry"][:, :], SB["hlast"][:, :], reads=[BF["hlast"]], writes=[ccB])
        op("gpsimd", lambda e: e.collective_compute(
            "AllGather", ALU.bypass, replica_groups=RG,
            ins=[DT["cin_carry"].opt()], outs=[DT["cout_carry"].opt()]),
            reads=[ccB], writes=[ccB])
        dma("sync", SB["carry"][:, :], DT["cout_carry"][0, :, :], reads=[ccB], writes=[BF["carry"]])
        op("vector", lambda e: e.tensor_scalar(
            SB["carry"][:, :], SB["carry"][:, :], SB["selmask"][:, 0:1], None, op0=ALU.mult),
            reads=[BF["carry"], BF["selmask"]], writes=[BF["carry"]])
        self.tap("g_hlast", SB["hlast"][:, :], [BF["hlast"]])

        self.barrier()
        self.close_scope(g1)
        if self.upto == "G1":
            self.close_scope(gall)
            return

        g3 = self.scope()
        garB = self.sb("g_gar", (P, 2, T), BF16, nbufs=2)
        xgB = self.sb("g_xg", (P, NFT, GP), F32, nbufs=NFT)
        xpB = self.sb("g_xp", (P, NFT, GP), F32, nbufs=NFT)
        n2bB = self.sb("g_n2b", (P, NFT, GP), BF16, nbufs=NFT)
        pmB = self.sb("g_pm", (P, NHT, GP), BF16, nbufs=NHT)
        ggB = self.sb("g_gg", (P, 2, GP), F32, nbufs=2)
        sqB = self.sb("g_sq", (P, 2, GP), BF16, nbufs=2)
        abB = self.sb("g_ab", (P, 2, GP), BF16, nbufs=2)
        self.sb("g_rb2", (1, GP), BF16); self.sb("g_rb3", (1, GP), BF16)
        self.sb("g_row1", (1, GP)); self.sb("g_row2", (1, GP)); self.sb("g_row3", (1, GP))
        g3rowB = BF["g_row1"]
        for n in ("g_row2", "g_row3", "g_rb2", "g_rb3"):
            BF[n] = g3rowB
        stWo = Prog.WStrip(self, "sWo", NHT, P)
        stMg = Prog.WStrip(self, "sMg", 8, P)
        stMu = Prog.WStrip(self, "sMu", 8, P)
        stMd = Prog.WStrip(self, "sMd", NHT, P)

        for m in range(NHT):
            sl = m % 2
            dma("sync", SB["g_gar"][:, sl, :], DT["gascr"][ts(m), :], writes=[garB[sl]])
            op("vector", lambda e, m=m, sl=sl: e.scalar_tensor_tensor(
                SB["g_prod"][:, m, :], SB["g_gar"][:, sl, :], SB["carry"][:, m:m + 1],
                SB["g_prod"][:, m, :], op0=ALU.mult, op1=ALU.add),
                reads=[garB[sl], BF["carry"], prodB[m]], writes=[prodB[m]])
        self.tap("g_prod", SB["g_prod"][:, :, :], prodB)

        for pp in range(NGP):
            p0 = pp * GP
            bcast("ps0", SB["mu_row"][:1, 3 + p0:3 + p0 + GP], rowsB, GP)
            bcast("ps1", SB["r_row"][:1, 3 + p0:3 + p0 + GP], rowsB, GP)
            for ft in range(NFT):
                fq = ft % 2
                mxb = "pa" if fq == 0 else "pb"
                slw = stWo.load("woutT", ft * P)
                mmg(mxb, [
                    (lambda e, k=k, slw=slw, p0=p0, mxb=mxb: e.matmul(
                        PS[mxb][:, :], stWo.ap(slw, k), SB["g_prod"][:, k, p0:p0 + GP],
                        start=k == 0, stop=k == NHT - 1))
                    for k in range(NHT)], [stWo.b(slw)] + prodB)
                dma("sync", SB["g_xp"][:, ft, :], DT["x"][ts(ft), p0 + 3:p0 + 3 + GP],
                    writes=[xpB[ft]])
                op("vector", lambda e, ft=ft: e.tensor_sub(
                    SB["g_xg"][:, ft, :], SB["g_xp"][:, ft, :], PS["ps0"][:, 0:GP]),
                    reads=[xpB[ft], BF["ps0"]], writes=[xgB[ft]])
                op("vector", lambda e, ft=ft: e.tensor_mul(
                    SB["g_xg"][:, ft, :], SB["g_xg"][:, ft, :], PS["ps1"][:, 0:GP]),
                    reads=[xgB[ft], BF["ps1"]], writes=[xgB[ft]])
                op("vector", lambda e, ft=ft, mxb=mxb: e.tensor_add(
                    SB["g_xg"][:, ft, :], SB["g_xg"][:, ft, :], PS[mxb][:, :]),
                    reads=[xgB[ft], BF[mxb]], writes=[xgB[ft]])
            for ft in range(NFT):
                fq = ft % 2
                op("scalar", lambda e, ft=ft, fq=fq: e.activation(
                    out=SB["g_sq"][:, fq, :], in_=SB["g_xg"][:, ft, :], func=AF.Square),
                    reads=[xgB[ft]], writes=[sqB[fq]])
                mmg("pc", [
                    (lambda e, ft=ft, fq=fq: e.matmul(
                        PS["pc"][0:1, :], SB["onesk"][:, :1], SB["g_sq"][:, fq, :],
                        start=ft == 0, stop=ft == NFT - 1))], [sqB[fq], BF["onesk"]])
            op("scalar", lambda e: e.activation(
                out=SB["g_row1"][:1, :], in_=PS["pc"][0:1, :], func=AF.Copy,
                scale=1.0 / 1024), reads=[BF["pc"]], writes=[g3rowB])
            op("scalar", lambda e: e.activation(
                out=SB["g_row1"][:1, :], in_=SB["g_row1"][:1, :], func=AF.Sqrt,
                bias=SB["epsc"][0:1, 0:1]), reads=[g3rowB, BF["epsc"]], writes=[g3rowB])
            op("vector", lambda e: e.reciprocal(
                SB["g_row2"][:1, :], SB["g_row1"][:1, :]), reads=[g3rowB], writes=[g3rowB])
            bcast("ps1", SB["g_row2"][:1, 0:GP], g3rowB, GP)
            for ft in range(NFT):
                op("vector", lambda e, ft=ft: e.scalar_tensor_tensor(
                    SB["g_n2b"][:, ft, :], SB["g_xg"][:, ft, :], SB["n2w"][:, ft:ft + 1],
                    PS["ps1"][:, 0:GP], op0=ALU.mult, op1=ALU.mult),
                    reads=[xgB[ft], BF["n2w"], BF["ps1"]], writes=[n2bB[ft]])
            for m in range(NHT):
                mq = m % 2
                gb_ = "pa" if mq == 0 else "pc"
                ub_ = "pb" if mq == 0 else "ps0"
                slg = stMg.load("mgateT", m * P)
                slu = stMu.load("mupT", m * P)
                mmg(gb_, [
                    (lambda e, k=k, slg=slg, gb_=gb_: e.matmul(
                        PS[gb_][:, 0:GP], stMg.ap(slg, k), SB["g_n2b"][:, k, :],
                        start=k == 0, stop=k == 7))
                    for k in range(8)], [stMg.b(slg)] + n2bB)
                op("scalar", lambda e, mq=mq, gb_=gb_: e.activation(
                    out=SB["g_gg"][:, mq, :], in_=PS[gb_][:, 0:GP], func=AF.Gelu),
                    reads=[BF[gb_]], writes=[ggB[mq]])
                mmg(ub_, [
                    (lambda e, k=k, slu=slu, ub_=ub_: e.matmul(
                        PS[ub_][:, 0:GP], stMu.ap(slu, k), SB["g_n2b"][:, k, :],
                        start=k == 0, stop=k == 7))
                    for k in range(8)], [stMu.b(slu)] + n2bB)
                op("vector", lambda e, m=m, mq=mq, ub_=ub_: e.tensor_mul(
                    SB["g_pm"][:, m, :], SB["g_gg"][:, mq, :], PS[ub_][:, 0:GP]),
                    reads=[ggB[mq], BF[ub_]], writes=[pmB[m]])
            for ft in range(NFT):
                fq = ft % 2
                dnb = "pa" if fq == 0 else "pb"
                sld = stMd.load("mdownT", ft * P)
                mmg(dnb, [
                    (lambda e, k=k, sld=sld, dnb=dnb: e.matmul(
                        PS[dnb][:, :], stMd.ap(sld, k), SB["g_pm"][:, k, :],
                        start=k == 0, stop=k == NHT - 1))
                    for k in range(NHT)], [stMd.b(sld)] + pmB)
                op("vector", lambda e, ft=ft, dnb=dnb: e.tensor_add(
                    SB["g_xg"][:, ft, :], SB["g_xg"][:, ft, :], PS[dnb][:, :]),
                    reads=[xgB[ft], BF[dnb]], writes=[xgB[ft]])
            if pp == 0:
                self.tap("g_A", SB["g_xg"][:, :, :], xgB)
            for ft in range(NFT):
                fq = ft % 2
                op("vector", lambda e, ft=ft, fq=fq: e.tensor_copy(
                    SB["g_ab"][:, fq, :], SB["g_xg"][:, ft, :]), reads=[xgB[ft]], writes=[abB[fq]])
                op("scalar", lambda e, ft=ft, fq=fq: e.activation(
                    out=SB["g_sq"][:, fq, :], in_=SB["g_xg"][:, ft, :], func=AF.Square),
                    reads=[xgB[ft]], writes=[sqB[fq]])
                mmg("pc", [
                    (lambda e, ft=ft, fq=fq: e.matmul(
                        PS["pc"][0:1, :], SB["onesk"][:, :1], SB["g_ab"][:, fq, :],
                        start=ft == 0, stop=ft == NFT - 1))], [abB[fq], BF["onesk"]])
                mmg("pb", [
                    (lambda e, ft=ft, fq=fq: e.matmul(
                        PS["pb"][0:1, :], SB["onesk"][:, :1], SB["g_sq"][:, fq, :],
                        start=ft == 0, stop=ft == NFT - 1))], [sqB[fq], BF["onesk"]])
            op("scalar", lambda e: e.activation(
                out=SB["g_row1"][:1, :], in_=PS["pc"][0:1, :], func=AF.Copy, scale=1.0 / 1024),
                reads=[BF["pc"]], writes=[g3rowB])
            op("scalar", lambda e: e.activation(
                out=SB["g_row2"][:1, :], in_=PS["pb"][0:1, :], func=AF.Copy, scale=1.0 / 1024),
                reads=[BF["pb"]], writes=[g3rowB])
            op("vector", lambda e: e.tensor_tensor(
                SB["g_row3"][:1, :], SB["g_row1"][:1, :], SB["g_row1"][:1, :], op=ALU.mult),
                reads=[g3rowB], writes=[g3rowB])
            op("vector", lambda e: e.tensor_sub(
                SB["g_row2"][:1, :], SB["g_row2"][:1, :], SB["g_row3"][:1, :]),
                reads=[g3rowB], writes=[g3rowB])
            op("scalar", lambda e: e.activation(
                out=SB["g_row2"][:1, :], in_=SB["g_row2"][:1, :], func=AF.Sqrt,
                bias=SB["epsc"][0:1, 0:1]), reads=[g3rowB, BF["epsc"]], writes=[g3rowB])
            op("vector", lambda e: e.reciprocal(
                SB["g_row2"][:1, :], SB["g_row2"][:1, :]), reads=[g3rowB], writes=[g3rowB])
            bcast("ps0", SB["g_row1"][:1, 0:GP], g3rowB, GP)
            bcast("ps1", SB["g_row2"][:1, 0:GP], g3rowB, GP)
            for ft in range(NFT):
                op("vector", lambda e, ft=ft: e.tensor_sub(
                    SB["g_xg"][:, ft, :], SB["g_xg"][:, ft, :], PS["ps0"][:, 0:GP]),
                    reads=[xgB[ft], BF["ps0"]], writes=[xgB[ft]])
                op("vector", lambda e, ft=ft: e.tensor_mul(
                    SB["g_xg"][:, ft, :], SB["g_xg"][:, ft, :], PS["ps1"][:, 0:GP]),
                    reads=[xgB[ft], BF["ps1"]], writes=[xgB[ft]])
                op("vector", lambda e, ft=ft: e.tensor_add(
                    SB["g_xg"][:, ft, :], SB["g_xg"][:, ft, :], SB["g_xp"][:, ft, :]),
                    reads=[xgB[ft], xpB[ft]], writes=[xgB[ft]])
                dma("sync", DT["x1scr"][ts(ft), p0:p0 + GP], SB["g_xg"][:, ft, :],
                    reads=[xgB[ft]])
            if pp == NGP - 1:
                self.tap("x1", SB["g_xg"][:, :, :], xgB)

        self.barrier()
        self.close_scope(g3)
        self.close_scope(gall)

    def phaseF(self):
        SB, PS, BF, DT = self.SB, self.PS, self.BF, self.DT
        op, mmg, dma, bcast = self.op, self.mmg, self.dma, self.bcast
        fsc = self.scope()
        HB = self.sb("f_H", (P, NFT, 1 + T), F32, nbufs=NFT)
        fhbB = self.sb("f_hb", (P, 2, 512), BF16, nbufs=2)
        fsqB = self.sb("f_sq", (P, 2, 512), BF16, nbufs=2)
        self.sb("f_row1", (1, 512)); self.sb("f_row2", (1, 512)); self.sb("f_row3", (1, 512))
        self.sb("f_rb", (1, 512), BF16)
        frB = BF["f_row1"]
        for n in ("f_row2", "f_row3", "f_rb"):
            BF[n] = frB

        def ln_stats_512(get_bf16, get_sq, n_in, mu_slot, r_slot, extra=1.0):
            for i in range(n_in):
                ai, bi, ri = get_bf16(i)
                mmg("pc", [(lambda e, ai=ai, i=i: e.matmul(
                    PS["pc"][0:1, :], SB["onesk"][:, :1], ai,
                    start=i == 0, stop=i == n_in - 1))], [ri, BF["onesk"]])
                sai, sri = get_sq(i)
                mmg("pb", [(lambda e, sai=sai, i=i: e.matmul(
                    PS["pb"][0:1, :], SB["onesk"][:, :1], sai,
                    start=i == 0, stop=i == n_in - 1))], [sri, BF["onesk"]])
            op("scalar", lambda e: e.activation(
                out=SB["f_row1"][:1, :], in_=PS["pc"][0:1, :], func=AF.Copy, scale=1.0 / 1024),
                reads=[BF["pc"]], writes=[frB])
            op("scalar", lambda e: e.activation(
                out=SB["f_row2"][:1, :], in_=PS["pb"][0:1, :], func=AF.Copy, scale=1.0 / 1024),
                reads=[BF["pb"]], writes=[frB])
            op("vector", lambda e: e.tensor_tensor(
                SB["f_row3"][:1, :], SB["f_row1"][:1, :], SB["f_row1"][:1, :], op=ALU.mult),
                reads=[frB], writes=[frB])
            op("vector", lambda e: e.tensor_sub(
                SB["f_row2"][:1, :], SB["f_row2"][:1, :], SB["f_row3"][:1, :]),
                reads=[frB], writes=[frB])
            op("scalar", lambda e: e.activation(
                out=SB["f_row2"][:1, :], in_=SB["f_row2"][:1, :], func=AF.Sqrt, bias=SB["epsc"][0:1, 0:1]),
                reads=[frB, BF["epsc"]], writes=[frB])
            op("vector", lambda e: e.reciprocal(
                SB["f_row2"][:1, :], SB["f_row2"][:1, :]), reads=[frB], writes=[frB])
            bcast("pa", SB["f_row1"][:1, 0:512], frB, 512)
            bcast("pb", SB["f_row2"][:1, 0:512], frB, 512)

        for fc in range(NFC):
            o0 = fc * FC
            for ft in range(NFT):
                dma("sync", SB["f_H"][:, ft, 1 + o0:1 + o0 + FC],
                    DT["x1scr"][ts(ft), o0:o0 + FC], writes=[HB[ft]])
            for o, w in npieces(FC):
                oo = o0 + o

                def getb(i, oo=oo):
                    ft = i
                    sl_ = i % 2
                    op("vector", lambda e, ft=ft, oo=oo, sl_=sl_: e.tensor_copy(
                        SB["f_hb"][:, sl_, :], SB["f_H"][:, ft, 1 + oo:1 + oo + 512]),
                        reads=[HB[ft]], writes=[fhbB[sl_]])
                    return SB["f_hb"][:, sl_, :], None, fhbB[sl_]

                def gets(i, oo=oo):
                    ft = i
                    sl_ = i % 2
                    op("scalar", lambda e, ft=ft, oo=oo, sl_=sl_: e.activation(
                        out=SB["f_sq"][:, sl_, :], in_=SB["f_H"][:, ft, 1 + oo:1 + oo + 512],
                        func=AF.Square), reads=[HB[ft]], writes=[fsqB[sl_]])
                    return SB["f_sq"][:, sl_, :], fsqB[sl_]
                ln_stats_512(getb, gets, NFT, 0, 1)
                for ft in range(NFT):
                    op("vector", lambda e, ft=ft, oo=oo: e.tensor_sub(
                        SB["f_H"][:, ft, 1 + oo:1 + oo + 512],
                        SB["f_H"][:, ft, 1 + oo:1 + oo + 512], PS["pa"][:, :]),
                        reads=[HB[ft], BF["pa"]], writes=[HB[ft]])
                    op("vector", lambda e, ft=ft, oo=oo: e.tensor_mul(
                        SB["f_H"][:, ft, 1 + oo:1 + oo + 512],
                        SB["f_H"][:, ft, 1 + oo:1 + oo + 512], PS["pb"][:, :]),
                        reads=[HB[ft], BF["pb"]], writes=[HB[ft]])
        for ft in range(NFT):
            op("vector", lambda e, ft=ft: e.memset(SB["f_H"][:, ft, 0:1], 0.0),
                reads=[], writes=[HB[ft]])
        self.tap("f_h1", SB["f_H"][:, :, :], HB)

        wkqB = self.sb("f_wkq", (P, 2, 512), F32, nbufs=2)
        wkq2B = self.sb("f_wkq2", (P, 2, 512), F32, nbufs=2)
        self.sb("f_hct", (P, NFT))
        hctB = BF["f_hct"]

        def emit_layer(l):
            SB = dict(self.SB)
            asc = self.scope()
            self.sb(f"f_pad{l}", (P, 16 + 16 * l), BF16)
            kB = self.sb("f_k", (P, 4, 2560), BF16, nbufs=4)
            vB = self.sb("f_v", (P, 20, 512), BF16, nbufs=20)
            qB = self.sb("f_q", (P, 4, T), BF16, nbufs=4)
            oB = self.sb("f_ow", (P, 4, 512), BF16, nbufs=4)
            self.sb("f_mask", (P, 4, 640), BF16)
            maskB = BF["f_mask"]
            stWoS = Prog.WStrip(self, "sWoS", 4, P)
            qsc = self.scope()
            self.sb("f_qraw", (P, 2, 512), F32, nbufs=2)
            qrawB = BF["f_qraw"]
            self.sb("f_qn", (P, 2, 512), BF16, nbufs=2)
            qnB = BF["f_qn"]
            self.sb("f_hbw", (P, NFT, 513), BF16, nbufs=NFT)
            hbwB = BF["f_hbw"]
            self.sb("f_wv", (P, 8, 512), BF16)
            wvB = BF["f_wv"]
            self.sb("f_tab", (P, 2, 512), F32, nbufs=2)
            tabB = BF["f_tab"]
            self.sb("f_nr", (2, 2, 512), F32, nbufs=2)
            nrB = BF["f_nr"]
            self.sb("f_nrb", (2, 2, 512), BF16, nbufs=2)
            nrbB = BF["f_nrb"]
            stQK = Prog.WStrip(self, "sQK", 8, P)
            SB = dict(self.SB)
            for kk in range(8):
                dma("gpsimd", SB["f_wv"][:, kk, :],
                    DT[f"wqkv{l}"][ts(kk), 1024:1536], writes=[wvB])

            for w in (3, 1, 2, 0):
                wbase = w * WIN
                for ft in range(NFT):
                    op("vector", lambda e, ft=ft, wbase=wbase: e.tensor_copy(
                        SB["f_hbw"][:, ft, :], SB["f_H"][:, ft, wbase:wbase + 513]),
                        reads=[HB[ft]], writes=[hbwB[ft]])
                def ldtab(isq, wbase=wbase):
                    tC = "tabCq" if isq else "tabCk"
                    tS = "tabSq" if isq else "tabSk"
                    sc1 = f"qs{l}" if isq else f"ks{l}"
                    sc2 = f"qssw{l}" if isq else f"kssw{l}"
                    dma("sync", SB["f_tab"][:, 0, :], DT[tC][:, wbase:wbase + 512],
                        writes=[tabB[0]])
                    dma("sync", SB["f_tab"][:, 1, :], DT[tS][:, wbase:wbase + 512],
                        writes=[tabB[1]])
                    op("vector", lambda e, sc1=sc1: e.tensor_scalar(
                        SB["f_tab"][:, 0, :], SB["f_tab"][:, 0, :], SB[sc1][:, 0:1],
                        None, op0=ALU.mult), reads=[tabB[0], BF[sc1]], writes=[tabB[0]])
                    op("vector", lambda e, sc2=sc2: e.tensor_scalar(
                        SB["f_tab"][:, 1, :], SB["f_tab"][:, 1, :], SB[sc2][:, 0:1],
                        None, op0=ALU.mult), reads=[tabB[1], BF[sc2]], writes=[tabB[1]])

                def qA(m):
                    sl = stQK.load(f"wqkv{l}", m * P)
                    psn = "ps0" if m % 2 == 0 else "ps1"
                    mmg(psn, [
                        (lambda e, k=k, sl=sl, psn=psn: e.matmul(
                            PS[psn][:, 0:512], stQK.ap(sl, k),
                            SB["f_hbw"][:, k, 1:513] if k < 4 else SB["f_hbw"][:, k, 0:512],
                            start=k == 0, stop=k == 7))
                        for k in range(8)], [stQK.b(sl)] + hbwB)

                def qB_(m):
                    psn = "ps0" if m % 2 == 0 else "ps1"
                    mp = m % 2
                    nbk = "pb" if m % 2 == 0 else "pc"
                    op("scalar", lambda e, psn=psn, mp=mp: e.activation(
                        out=SB["f_qraw"][:, mp, :], in_=PS[psn][:, 0:512], func=AF.Copy),
                        reads=[BF[psn]], writes=[qrawB[mp]])
                    op("scalar", lambda e, mp=mp: e.activation(
                        out=SB["f_sq"][:, mp, :], in_=SB["f_qraw"][:, mp, :], func=AF.Square),
                        reads=[qrawB[mp]], writes=[fsqB[mp]])
                    mmg(nbk, [(lambda e, nbk=nbk, mp=mp: e.matmul(
                        PS[nbk][0:2, :], SB["blockones"][:, :], SB["f_sq"][:, mp, :],
                        start=True, stop=True))], [fsqB[mp], BF["blockones"]])
                    op("scalar", lambda e, mp=mp, nbk=nbk: e.activation(
                        out=SB["f_nr"][:2, mp, :], in_=PS[nbk][0:2, :], func=AF.Sqrt),
                        reads=[BF[nbk]], writes=[nrB[mp]])
                    op("vector", lambda e, mp=mp: e.tensor_scalar_max(
                        SB["f_nr"][:2, mp, :], SB["f_nr"][:2, mp, :], 1e-12),
                        reads=[nrB[mp]], writes=[nrB[mp]])
                    op("vector", lambda e, mp=mp: e.reciprocal(
                        SB["f_nr"][:2, mp, :], SB["f_nr"][:2, mp, :]),
                        reads=[nrB[mp]], writes=[nrB[mp]])
                    op("vector", lambda e, mp=mp: e.tensor_copy(
                        SB["f_nrb"][:2, mp, :], SB["f_nr"][:2, mp, :]),
                        reads=[nrB[mp]], writes=[nrbB[mp]])

                def qC(m):
                    mp = m % 2
                    nbk = "pb" if m % 2 == 0 else "pc"
                    mmg(nbk, [(lambda e, mp=mp, nbk=nbk: e.matmul(
                        PS[nbk][:, :], SB["pat2"][:2, :], SB["f_nrb"][:2, mp, :],
                        start=True, stop=True))], [nrbB[mp], BF["pat2"]])
                    op("vector", lambda e, mp=mp, nbk=nbk: e.tensor_mul(
                        SB["f_qn"][:, mp, :], SB["f_qraw"][:, mp, :], PS[nbk][:, :]),
                        reads=[qrawB[mp], BF[nbk]], writes=[qnB[mp]])
                    mmg(nbk, [(lambda e, mp=mp, nbk=nbk: e.matmul(
                        PS[nbk][:, :], SB["perm"][:, :], SB["f_qn"][:, mp, :],
                        start=True, stop=True))], [qnB[mp], BF["perm"]])

                def qD(m, wbase=wbase):
                    mp = m % 2
                    nbk = "pb" if m % 2 == 0 else "pc"
                    isq = m < 4
                    tp = m % 4
                    op("vector", lambda e, mp=mp: e.tensor_mul(
                        SB["f_wkq"][:, mp, :], SB["f_qn"][:, mp, :], SB["f_tab"][:, 0, :]),
                        reads=[qnB[mp], tabB[0]], writes=[wkqB[mp]])
                    op("vector", lambda e, mp=mp, nbk=nbk: e.tensor_mul(
                        SB["f_wkq2"][:, mp, :], PS[nbk][:, :], SB["f_tab"][:, 1, :]),
                        reads=[BF[nbk], tabB[1]], writes=[wkq2B[mp]])
                    if isq:
                        op("vector", lambda e, tp=tp, mp=mp, wbase=wbase: e.tensor_add(
                            SB["f_q"][:, tp, wbase:wbase + 512], SB["f_wkq"][:, mp, :],
                            SB["f_wkq2"][:, mp, :]),
                            reads=[wkqB[mp], wkq2B[mp]], writes=[qB[tp]])
                    else:
                        op("vector", lambda e, tp=tp, mp=mp, wbase=wbase: e.tensor_add(
                            SB["f_k"][:, tp, 512 + wbase:1024 + wbase],
                            SB["f_wkq"][:, mp, :], SB["f_wkq2"][:, mp, :]),
                            reads=[wkqB[mp], wkq2B[mp]], writes=[kB[tp]])

                ldtab(True)
                for step in range(4 + 3):
                    if 3 <= step:
                        qD(step - 3)
                    if 2 <= step and step - 2 < 4:
                        qC(step - 2)
                    if 1 <= step and step - 1 < 4:
                        qB_(step - 1)
                    if step < 4:
                        qA(step)
                ldtab(False)
                for step in range(4 + 3):
                    if 3 <= step:
                        qD(4 + step - 3)
                    if 2 <= step and step - 2 < 4:
                        qC(4 + step - 2)
                    if 1 <= step and step - 1 < 4:
                        qB_(4 + step - 1)
                    if step < 4:
                        qA(4 + step)
                for tt in range(4):
                    vt = 4 * w + tt + 4
                    mmg("pa", [
                        (lambda e, kk=kk, tt=tt: e.matmul(
                            PS["pa"][:, :], SB["f_hbw"][:, kk, 1 + tt * P:1 + tt * P + P]
                            if kk < 4 else SB["f_hbw"][:, kk, tt * P:tt * P + P],
                            SB["f_wv"][:, kk, :], start=kk == 0, stop=kk == 7))
                        for kk in range(8)], hbwB + [wvB])
                    op("scalar", lambda e, vt=vt: e.activation(
                        out=SB["f_v"][:, vt, :], in_=PS["pa"][:, :], func=AF.Copy),
                        reads=[BF["pa"]], writes=[vB[vt]])
                if w == 3:
                    kvB = self.buf(f"kv{l}")
                    for i in range(4):
                        dma("sync", DT[f"cin_kv{l}"][i, :, :], SB["f_k"][:, i, 2048:2560],
                            reads=[kB[i]], writes=[kvB])
                    for i in range(4):
                        dma("sync", DT[f"cin_kv{l}"][4 + i, :, :], SB["f_v"][:, 16 + i, :],
                            reads=[vB[16 + i]], writes=[kvB])
                    hcB = self.buf(f"hc{l}")
                    for ft in range(NFT):
                        dma("sync", DT[f"cin_hc{l}"][ft:ft + 1, :].rearrange("o p -> p o"),
                            SB["f_H"][:, ft, T:T + 1], reads=[HB[ft]], writes=[hcB])
                    op("gpsimd", lambda e, l=l: e.collective_compute(
                        "AllGather", ALU.bypass, replica_groups=RG,
                        ins=[DT[f"cin_kv{l}"].opt()], outs=[DT[f"cout_kv{l}"].opt()]),
                        reads=[kvB], writes=[kvB])
                    op("gpsimd", lambda e, l=l: e.collective_compute(
                        "AllGather", ALU.bypass, replica_groups=RG,
                        ins=[DT[f"cin_hc{l}"].opt()], outs=[DT[f"cout_hc{l}"].opt()]),
                        reads=[hcB], writes=[hcB])
                    for i in range(4):
                        dma("sync", SB["f_k"][:, i, 0:512], DT[f"cout_kv{l}"][0, i, :, :],
                            reads=[kvB], writes=[kB[i]])
                    for i in range(4):
                        dma("sync", SB["f_v"][:, i, :], DT[f"cout_kv{l}"][0, 4 + i, :, :],
                            reads=[kvB], writes=[vB[i]])
                    for ft in range(NFT):
                        dma("sync", SB["f_hct"][:, ft:ft + 1],
                            DT[f"cout_hc{l}"][0, ft:ft + 1, :].rearrange("o p -> p o"),
                            reads=[hcB], writes=[hctB])
                    op("vector", lambda e: e.tensor_scalar(
                        SB["f_hct"][:, :], SB["f_hct"][:, :], SB["selmask"][:, 0:1],
                        None, op0=ALU.mult), reads=[hctB, BF["selmask"]], writes=[hctB])
                    for ft in range(NFT):
                        op("vector", lambda e, ft=ft: e.tensor_copy(
                            SB["f_H"][:, ft, 0:1], SB["f_hct"][:, ft:ft + 1]),
                            reads=[hctB], writes=[HB[ft]])
                else:
                    pass
            if l == 0:
                self.tap("f_q", SB["f_q"][:, :, :], qB)
                self.tap("f_k", SB["f_k"][:, :, :], kB)
                self.tap("f_v", SB["f_v"][:, :, :], vB)

            self.barrier()
            self.close_scope(qsc)
            self.sb("f_p", (P, 4, 640), BF16, nbufs=4)
            pB = BF["f_p"]
            self.sb("f_rect", (P, 4, 8, 512), BF16, nbufs=4)
            rectB = BF["f_rect"]
            self.sb("f_ell", (P, 4), F32, nbufs=4)
            ellB = BF["f_ell"]
            self.sb("f_negm", (P, 4), F32, nbufs=4)
            negmB = BF["f_negm"]
            SB = dict(self.SB)
            for rr in range(4):
                op("vector", lambda e, rr=rr: e.memset(SB["f_rect"][:, rr, :, :], 0.0),
                    writes=[rectB[rr]])

            for w in range(NW):
                mname = "maskA" if w == 0 else "maskB"
                dma("gpsimd", SB["f_mask"][:, :, :],
                    DT[mname].rearrange("q p m -> p q m"), writes=[maskB])
                for tp in range(4):
                    rpar = 2 * (tp % 2)
                    items = [(qc, hh) for qc in range(4) for hh in range(2)]

                    def stA(i, tp=tp, w=w):
                        qc, hh = items[i]
                        q0 = qc * P
                        band0 = w * WIN + qc * P
                        psn = "ps0" if hh == 0 else "ps1"
                        hof = 64 * hh
                        mmg(psn, [
                            (lambda e, o=o, wd=wd, hof=hof, tp=tp, q0=q0, band0=band0,
                             psn=psn, w=w: e.matmul(
                                PS[psn][:, o:o + wd],
                                SB["f_q"][hof:hof + 64, tp, w * WIN + q0:w * WIN + q0 + P],
                                SB["f_k"][hof:hof + 64, tp, band0 + o:band0 + o + wd],
                                start=True, stop=True,
                                tile_position=(hof, 0)))
                            for o, wd in npieces(640)],
                            [qB[tp], kB[tp]])

                    def stB(i):
                        qc, hh = items[i]
                        psn = "ps0" if hh == 0 else "ps1"
                        psl = i % 4
                        op("vector", lambda e, psn=psn, qc=qc: e.tensor_tensor(
                            PS[psn][:, :], PS[psn][:, :], SB["f_mask"][:, qc, :],
                            op=ALU.add), reads=[BF[psn], maskB], writes=[BF[psn]])
                        op("vector", lambda e, psn=psn, psl=psl: e.tensor_reduce(
                            SB["f_negm"][:, psl:psl + 1], PS[psn][:, :], axis=AX.X,
                            op=ALU.max, negate=True), reads=[BF[psn]], writes=[negmB[psl]])

                    def stC(i):
                        qc, hh = items[i]
                        psn = "ps0" if hh == 0 else "ps1"
                        psl = i % 4
                        op("scalar", lambda e, psn=psn, psl=psl: e.activation(
                            out=SB["f_p"][:, psl, :], in_=PS[psn][:, :], func=AF.Exp,
                            bias=SB["f_negm"][:, psl:psl + 1],
                            accum_out=SB["f_ell"][:, psl:psl + 1]),
                            reads=[BF[psn], negmB[psl]], writes=[pB[psl], ellB[psl]])
                        op("vector", lambda e, psl=psl: e.reciprocal(
                            SB["f_ell"][:, psl:psl + 1], SB["f_ell"][:, psl:psl + 1]),
                            reads=[ellB[psl]], writes=[ellB[psl]])

                    def stD(i):
                        psl = i % 4
                        op("scalar", lambda e, psl=psl: e.activation(
                            out=SB["f_p"][:, psl, :], in_=SB["f_p"][:, psl, :],
                            func=AF.Copy, scale=SB["f_ell"][:, psl:psl + 1]),
                            reads=[pB[psl], ellB[psl]], writes=[pB[psl]])

                    def stE(i, rpar=rpar):
                        qc, hh = items[i]
                        psl = i % 4
                        rsl = hh + rpar
                        q0 = qc * P
                        for b in range(4):
                            op("tensor", lambda e, psl=psl, b=b: e.transpose(
                                PS["pt"][:, b * P:(b + 1) * P],
                                SB["f_p"][:, psl, b * P:(b + 1) * P],
                                SB["ident"][:, :]),
                                reads=[pB[psl], BF["ident"]], writes=[BF[f"pt{b}"]])
                        op("scalar", lambda e, rsl=rsl, qc=qc, q0=q0: e.activation(
                            out=SB["f_rect"][:, rsl, qc:qc + 4, q0:q0 + P],
                            in_=PS["pt"][:, 0:512].rearrange("p (b q) -> p b q", b=4),
                            func=AF.Copy),
                            reads=[BF["pt0"], BF["pt1"], BF["pt2"], BF["pt3"]],
                            writes=[rectB[rsl]])
                        op("tensor", lambda e, psl=psl: e.transpose(
                            PS["pt"][:, 0:P], SB["f_p"][:, psl, 4 * P:5 * P],
                            SB["ident"][:, :]),
                            reads=[pB[psl], BF["ident"]], writes=[BF["pt0"]])
                        op("scalar", lambda e, rsl=rsl, qc=qc, q0=q0: e.activation(
                            out=SB["f_rect"][:, rsl, qc + 4, q0:q0 + P],
                            in_=PS["pt"][:, 0:P], func=AF.Copy),
                            reads=[BF["pt0"]], writes=[rectB[rsl]])

                    NI = len(items)
                    for step in range(NI + 4):
                        if 4 <= step and step - 4 < NI:
                            stE(step - 4)
                        if 3 <= step and step - 3 < NI:
                            stD(step - 3)
                        if 2 <= step and step - 2 < NI:
                            stC(step - 2)
                        if 1 <= step and step - 1 < NI:
                            stB(step - 1)
                        if step < NI:
                            stA(step)
                    pvb = "pa" if tp % 2 == 0 else "pc"
                    mmg(pvb, [
                        (lambda e, kc=kc, hh=hh, w=w, tp=tp, pvb=pvb: e.matmul(
                            PS[pvb][64 * hh:64 * hh + 64, :],
                            SB["f_v"][:, 4 * w + kc, 64 * (2 * tp + hh):64 * (2 * tp + hh) + 64],
                            SB["f_rect"][:, hh + 2 * (tp % 2), kc, :],
                            start=kc == 0, stop=kc == 7, tile_position=(0, 64 * hh)))
                        for hh in range(2) for kc in range(8)],
                        [vB[4 * w + kc2] for kc2 in range(8)] + [rectB[rpar], rectB[rpar + 1]])
                    op("scalar", lambda e, tp=tp, pvb=pvb: e.activation(
                        out=SB["f_ow"][:, tp, :], in_=PS[pvb][:, :], func=AF.Copy),
                        reads=[BF[pvb]], writes=[oB[tp]])
                for ft in range(NFT):
                    slw = stWoS.load(f"wo{l}", ft * P)
                    mmg("pb", [
                        (lambda e, kt=kt, slw=slw: e.matmul(
                            PS["pb"][:, :], stWoS.ap(slw, kt), SB["f_ow"][:, kt, :],
                            start=kt == 0, stop=kt == 3))
                        for kt in range(4)], [stWoS.b(slw)] + oB)
                    op("vector", lambda e, ft=ft, w=w: e.tensor_add(
                        SB["f_H"][:, ft, 1 + w * WIN:1 + w * WIN + 512],
                        SB["f_H"][:, ft, 1 + w * WIN:1 + w * WIN + 512], PS["pb"][:, :]),
                        reads=[HB[ft], BF["pb"]], writes=[HB[ft]])
            if l == 0:
                self.tap("f_o", SB["f_ow"][:, :, :], oB)
                self.tap("f_pa", SB["f_H"][:, :, :], HB)

            paB = self.buf(f"pa{l}")
            for ft in range(NFT):
                dma("sync", DT[f"cin_pa{l}"][ft:ft + 1, :].rearrange("o p -> p o"),
                    SB["f_H"][:, ft, T:T + 1], reads=[HB[ft]], writes=[paB])
            op("gpsimd", lambda e, l=l: e.collective_compute(
                "AllGather", ALU.bypass, replica_groups=RG,
                ins=[DT[f"cin_pa{l}"].opt()], outs=[DT[f"cout_pa{l}"].opt()]),
                reads=[paB], writes=[paB])
            for ft in range(NFT):
                dma("sync", SB["f_hct"][:, ft:ft + 1],
                    DT[f"cout_pa{l}"][0, ft:ft + 1, :].rearrange("o p -> p o"),
                    reads=[paB], writes=[hctB])
            op("vector", lambda e: e.tensor_scalar(
                SB["f_hct"][:, :], SB["f_hct"][:, :], SB["selmask"][:, 0:1],
                None, op0=ALU.mult), reads=[hctB, BF["selmask"]], writes=[hctB])
            for ft in range(NFT):
                op("vector", lambda e, ft=ft: e.tensor_copy(
                    SB["f_H"][:, ft, 0:1], SB["f_hct"][:, ft:ft + 1]),
                    reads=[hctB], writes=[HB[ft]])

            self.barrier()
            self.close_scope(asc)
            fsc2 = self.scope()
            self.sb(f"f_pad2{l}", (P, 48 + 16 * l), BF16)
            hnB = self.sb("f_hn", (P, NFT, T), BF16, nbufs=NFT)
            prodB = self.sb("f_prod", (P, MT1P, FC), BF16, nbufs=MT1P)
            fggB = self.sb("f_gg", (P, 2, 512), F32, nbufs=2)
            stW1 = Prog.WStrip(self, "sW1", 8, 256)
            stW2 = Prog.WStrip(self, "sW2", KT2, P)
            SB = dict(self.SB)

            for fc in (1, 0):
                o0 = fc * FC
                for o, wdt in npieces(FC):
                    oo = o0 + o

                    def getb(i, oo=oo):
                        ft = i
                        sl_ = i % 2
                        sh = 1 if ft < 4 else 0
                        op("vector", lambda e, ft=ft, oo=oo, sh=sh, sl_=sl_: e.tensor_copy(
                            SB["f_hb"][:, sl_, :], SB["f_H"][:, ft, sh + oo:sh + oo + 512]),
                            reads=[HB[ft]], writes=[fhbB[sl_]])
                        return SB["f_hb"][:, sl_, :], None, fhbB[sl_]

                    def gets(i, oo=oo):
                        ft = i
                        sl_ = i % 2
                        sh = 1 if ft < 4 else 0
                        op("scalar", lambda e, ft=ft, oo=oo, sh=sh, sl_=sl_: e.activation(
                            out=SB["f_sq"][:, sl_, :], in_=SB["f_H"][:, ft, sh + oo:sh + oo + 512],
                            func=AF.Square), reads=[HB[ft]], writes=[fsqB[sl_]])
                        return SB["f_sq"][:, sl_, :], fsqB[sl_]
                    ln_stats_512(getb, gets, NFT, 0, 1)
                    for ft in range(NFT):
                        sh = 1 if ft < 4 else 0
                        wsl = ft % 2
                        op("vector", lambda e, ft=ft, oo=oo, sh=sh, wsl=wsl: e.tensor_sub(
                            SB["f_wkq"][:, wsl, :], SB["f_H"][:, ft, sh + oo:sh + oo + 512],
                            PS["pa"][:, :]), reads=[HB[ft], BF["pa"]], writes=[wkqB[wsl]])
                        op("vector", lambda e, wsl=wsl: e.tensor_mul(
                            SB["f_wkq"][:, wsl, :], SB["f_wkq"][:, wsl, :], PS["pb"][:, :]),
                            reads=[wkqB[wsl], BF["pb"]], writes=[wkqB[wsl]])
                        op("vector", lambda e, ft=ft, oo=oo, wsl=wsl: e.tensor_scalar(
                            SB["f_hn"][:, ft, oo:oo + 512], SB["f_wkq"][:, wsl, :],
                            SB[f"lng{l}"][:, ft:ft + 1], SB[f"lnb{l}"][:, ft:ft + 1],
                            op0=ALU.mult, op1=ALU.add),
                            reads=[wkqB[wsl], BF[f"lng{l}"], BF[f"lnb{l}"]],
                            writes=[hnB[ft]])
            for fc in (1, 0):
                o0 = fc * FC
                for j in range(MT1P):
                    jq = j % 2
                    ab_ = "pa" if jq == 0 else "pc"
                    gb2 = "pb" if jq == 0 else "ps0"
                    sl = stW1.load(f"w1p{l}", j * 256)
                    for o, wdt in npieces(FC):
                        mmg(ab_, [
                            (lambda e, k=k, sl=sl, o=o, o0=o0, ab_=ab_: e.matmul(
                                PS[ab_][:, 0:512], stW1.ap(sl, k, slice(0, 128)),
                                SB["f_hn"][:, k, o0 + o:o0 + o + 512], start=k == 0, stop=k == 7))
                            for k in range(8)], [stW1.b(sl)] + hnB)
                        mmg(gb2, [
                            (lambda e, k=k, sl=sl, o=o, o0=o0, gb2=gb2: e.matmul(
                                PS[gb2][:, 0:512], stW1.ap(sl, k, slice(128, 256)),
                                SB["f_hn"][:, k, o0 + o:o0 + o + 512], start=k == 0, stop=k == 7))
                            for k in range(8)], [stW1.b(sl)] + hnB)
                        op("scalar", lambda e, j=j, jq=jq, gb2=gb2: e.activation(
                            out=SB["f_gg"][:, jq, :], in_=PS[gb2][:, 0:512], func=AF.Gelu,
                            bias=SB[f"b1p{l}"][:, 2 * j + 1:2 * j + 2]),
                            reads=[BF[gb2], BF[f"b1p{l}"]], writes=[fggB[jq]])
                        op("vector", lambda e, j=j, jq=jq, o=o, ab_=ab_: e.scalar_tensor_tensor(
                            SB["f_prod"][:, j, o:o + 512], PS[ab_][:, 0:512],
                            SB[f"b1p{l}"][:, 2 * j:2 * j + 1], SB["f_gg"][:, jq, :],
                            op0=ALU.add, op1=ALU.mult),
                            reads=[BF[ab_], BF[f"b1p{l}"], fggB[jq]], writes=[prodB[j]])
                for ft in range(NFT):
                    fq = ft % 2
                    w2b = "pa" if fq == 0 else "pb"
                    sl = stW2.load(f"w2p{l}", ft * P)
                    for o, wdt in npieces(FC):
                        mmg(w2b, [
                            (lambda e, k=k, sl=sl, o=o, w2b=w2b: e.matmul(
                                PS[w2b][:, :], stW2.ap(sl, k), SB["f_prod"][:, k, o:o + 512],
                                start=k == 0, stop=k == KT2 - 1))
                            for k in range(KT2)], [stW2.b(sl)] + prodB)
                        op("vector", lambda e, ft=ft, o0=o0, o=o, w2b=w2b: e.scalar_tensor_tensor(
                            SB["f_H"][:, ft, 1 + o0 + o:1 + o0 + o + 512], PS[w2b][:, :],
                            SB[f"b2{l}"][:, ft:ft + 1],
                            SB["f_H"][:, ft, 1 + o0 + o:1 + o0 + o + 512],
                            op0=ALU.add, op1=ALU.add),
                            reads=[BF[w2b], BF[f"b2{l}"], HB[ft]], writes=[HB[ft]])
            if l == 0:
                self.tap("f_out", SB["f_H"][:, :, :], HB)
            self.barrier()
            self.close_scope(fsc2)

        for l in range(2):
            emit_layer(l)

        x1B = self.sb("f_x1", (P, NFT, 512), F32, nbufs=NFT)
        outB = self.sb("f_yt", (P, 512), F32)
        for o in range(0, T, 512):
            def getb(i, o=o):
                ft = i
                sl_ = i % 2
                op("vector", lambda e, ft=ft, o=o, sl_=sl_: e.tensor_copy(
                    SB["f_hb"][:, sl_, :], SB["f_H"][:, ft, 1 + o:1 + o + 512]),
                    reads=[HB[ft]], writes=[fhbB[sl_]])
                return SB["f_hb"][:, sl_, :], None, fhbB[sl_]

            def gets(i, o=o):
                ft = i
                sl_ = i % 2
                op("scalar", lambda e, ft=ft, o=o, sl_=sl_: e.activation(
                    out=SB["f_sq"][:, sl_, :], in_=SB["f_H"][:, ft, 1 + o:1 + o + 512],
                    func=AF.Square), reads=[HB[ft]], writes=[fsqB[sl_]])
                return SB["f_sq"][:, sl_, :], fsqB[sl_]
            ln_stats_512(getb, gets, NFT, 0, 1)
            for ft in range(NFT):
                dma("sync", SB["f_x1"][:, ft, :], DT["x1scr"][ts(ft), o:o + 512],
                    writes=[x1B[ft]])
                op("vector", lambda e, ft=ft, o=o: e.tensor_sub(
                    SB["f_yt"][:, :], SB["f_H"][:, ft, 1 + o:1 + o + 512], PS["pa"][:, :]),
                    reads=[HB[ft], BF["pa"]], writes=[BF["f_yt"]])
                op("vector", lambda e: e.tensor_mul(
                    SB["f_yt"][:, :], SB["f_yt"][:, :], PS["pb"][:, :]),
                    reads=[BF["f_yt"], BF["pb"]], writes=[BF["f_yt"]])
                op("vector", lambda e, ft=ft: e.tensor_add(
                    SB["f_yt"][:, :], SB["f_yt"][:, :], SB["f_x1"][:, ft, :]),
                    reads=[BF["f_yt"], x1B[ft]], writes=[BF["f_yt"]])
                dma("sync", DT["y"][ts(ft), o:o + 512], SB["f_yt"][:, :],
                    reads=[BF["f_yt"]])
        self.barrier()
        self.close_scope(fsc)


def build_program(taps=(), upto=None):
    return Prog(taps, upto=upto).build()


def kernel(**inputs):
    shared, per_core = _prep_inputs(inputs)
    nc = build_program()
    in_maps = []
    for c in range(8):
        d = dict(shared)
        d.update(per_core[c])
        in_maps.append(d)
    res = run_bass_kernel_spmd(nc, in_maps, core_ids=list(range(8)))
    out = np.zeros((4, 4096, 1024), np.float32)
    for c in range(8):
        b, half = c // 2, c % 2
        out[b, half * T:(half + 1) * T, :] = res.results[c]["y"].T
    return out


# version 09
# speedup vs baseline: 1.0616x; 1.0616x over previous
import numpy as np
import concourse.bass as bass
import concourse.mybir as mybir
from concourse.bass_utils import run_bass_kernel_spmd

F32 = mybir.dt.float32
BF16 = mybir.dt.bfloat16
AF = mybir.ActivationFunctionType
ALU = mybir.AluOpType
AX = mybir.AxisListType

P = 128
T = 2048
WG = T + 3
NFT = 8
NHT = 12
TC = 512
GC = T // TC
GP = 512
NGP = T // GP
FC = 1024
NFC = T // FC
WIN = 512
NW = T // WIN
MT1P = 22
KT2 = 22
NEG = -3.0e38
EPS = 1e-5
NDMASLOT = 8
RG = [[0, 1], [2, 3], [4, 5], [6, 7]]



def _t(a):
    return np.ascontiguousarray(np.asarray(a, np.float32).T)


def _cvec(v, n_tiles):
    return np.ascontiguousarray(np.asarray(v, np.float32).reshape(n_tiles, P).T)


def _rot_tables(t0):
    DH = 64
    n = 4096
    half = np.arange(0, DH, 2, dtype=np.float64)
    inv_freq = 1.0 / 10000.0 ** (half / DH)
    t = np.arange(t0, t0 + T, dtype=np.float64)
    fr = t[None, :] * inv_freq[:, None]
    base = (half + 0.4 * DH) / (1.4 * DH)
    power = (t - n // 2) / float(WIN // 2)
    sc = base[:, None] ** power[None, :]
    cos, sin = np.cos(fr), np.sin(fr)
    QKS = 8.0
    Cq = np.concatenate([cos * sc, cos * sc], 0) * QKS
    Sq = np.concatenate([-sin * sc, sin * sc], 0) * QKS
    Ck = np.concatenate([cos / sc, cos / sc], 0)
    Sk = np.concatenate([-sin / sc, sin / sc], 0)
    return [np.ascontiguousarray(np.tile(a, (2, 1)).astype(np.float32))
            for a in (Cq, Sq, Ck, Sk)]


def _mask_tiles(seq_start):
    m = np.zeros((4, P, 640), np.float32)
    r = np.arange(P)[:, None]
    cc = np.arange(P)[None, :]
    for qc in range(4):
        for b in range(5):
            imj = 512 - 128 * b + r - cc
            bad = (imj < 0) | (imj > 512)
            if seq_start:
                bad = bad | ((qc * 128 - 512 + 128 * b + cc + 0 * r) < 0)
            m[qc][:, b * 128:(b + 1) * 128][bad] = NEG
    return m


def _prep_inputs(inputs):
    f32 = np.float32
    x = np.asarray(inputs["L0_attn_x"], f32)
    sh = {}
    sh["winT"] = _t(inputs["L0_attn_W_in"])
    sh["wgateT"] = _t(inputs["L0_attn_W_gate"])
    sh["rginT"] = _t(inputs["L0_attn_rg_in_w"])
    sh["rggateT"] = _t(inputs["L0_attn_rg_gate_w"])
    sh["woutT"] = _t(inputs["L0_attn_W_out"])
    sh["mgateT"] = _t(inputs["L0_attn_mlp_gate_w"])
    sh["mupT"] = _t(inputs["L0_attn_mlp_up_w"])
    sh["mdownT"] = _t(inputs["L0_attn_mlp_down_w"])
    cw = np.asarray(inputs["L0_attn_conv_w"], f32)[:, 0, :]
    sh["convw"] = np.ascontiguousarray(cw.reshape(NHT, P, 4).transpose(1, 0, 2).reshape(P, NHT * 4))
    sh["convb"] = _cvec(inputs["L0_attn_conv_b"], NHT)
    sh["rgib"] = _cvec(inputs["L0_attn_rg_in_b"], NHT)
    sh["rggb"] = _cvec(inputs["L0_attn_rg_gate_b"], NHT)
    sh["flam"] = _cvec(inputs["L0_attn_forget_lambda"], NHT)
    sh["n1w"] = _cvec(inputs["L0_attn_norm1_w"], NFT)
    sh["n2w"] = _cvec(inputs["L0_attn_norm2_w"], NFT)
    for l in range(2):
        sh[f"wqkv{l}"] = np.ascontiguousarray(np.asarray(inputs["L0_ffn_wqkv"], f32)[l])
        sh[f"wo{l}"] = np.ascontiguousarray(np.asarray(inputs["L0_ffn_wo"], f32)[l])
        w1 = np.asarray(inputs["L0_ffn_w1"], f32)[l]
        b1 = np.asarray(inputs["L0_ffn_b1"], f32)[l]
        a = np.concatenate([w1[:, :2730], np.zeros((1024, 86), f32)], 1).reshape(1024, 22, 128)
        g = np.concatenate([w1[:, 2730:], np.zeros((1024, 86), f32)], 1).reshape(1024, 22, 128)
        sh[f"w1p{l}"] = np.ascontiguousarray(np.stack([a, g], 2).reshape(1024, 44 * 128))
        ba = np.concatenate([b1[:2730], np.zeros(86, f32)]).reshape(22, 128)
        bg = np.concatenate([b1[2730:], np.zeros(86, f32)]).reshape(22, 128)
        sh[f"b1p{l}"] = np.ascontiguousarray(np.stack([ba, bg], 1).reshape(44, 128).T)
        w2 = np.asarray(inputs["L0_ffn_w2"], f32)[l]
        sh[f"w2p{l}"] = np.ascontiguousarray(np.concatenate([w2, np.zeros((86, 1024), f32)], 0))
        sh[f"b2{l}"] = _cvec(np.asarray(inputs["L0_ffn_b2"], f32)[l], NFT)
        sh[f"lng{l}"] = _cvec(np.asarray(inputs["L0_ffn_ln_g"], f32)[l], NFT)
        sh[f"lnb{l}"] = _cvec(np.asarray(inputs["L0_ffn_ln_b"], f32)[l], NFT)
        qs = np.asarray(inputs["L0_ffn_q_scale"], f32)[l]
        ks = np.asarray(inputs["L0_ffn_k_scale"], f32)[l]
        idx = np.arange(P) % 64
        swi = (idx + 32) % 64
        sh[f"qs{l}"] = np.ascontiguousarray(qs[idx][:, None])
        sh[f"qssw{l}"] = np.ascontiguousarray(qs[swi][:, None])
        sh[f"ks{l}"] = np.ascontiguousarray(ks[idx][:, None])
        sh[f"kssw{l}"] = np.ascontiguousarray(ks[swi][:, None])
    sh["ident"] = np.eye(P, dtype=f32)
    pm = np.zeros((P, P), f32)
    for p in range(P):
        pm[(p % 64 + 32) % 64 + 64 * (p // 64), p] = 1.0
    sh["perm"] = pm
    sh["onescol"] = np.ones((1, P), f32)
    sh["onesk"] = np.ones((P, 1), f32)
    sh["epsc"] = np.full((P, 1), 1e-5, f32)
    pat2 = np.zeros((2, P), f32)
    pat2[0, :64] = 1.0
    pat2[1, 64:] = 1.0
    sh["pat2"] = pat2
    bo = np.zeros((P, 2), f32)
    bo[:64, 0] = 1.0
    bo[64:, 1] = 1.0
    sh["blockones"] = bo
    sh["maskB"] = _mask_tiles(False)
    mask_a = _mask_tiles(True)
    per_core = []
    for c in range(8):
        half = c % 2
        b = c // 2
        t0 = half * T
        d = {}
        xc = np.zeros((1024, WG), f32)
        xc[:, 3:] = x[b, t0:t0 + T, :].T
        if half == 1:
            xc[:, :3] = x[b, t0 - 3:t0, :].T
        d["x"] = np.ascontiguousarray(xc)
        d["selmask"] = np.full((P, 1), 1.0 if half == 1 else 0.0, f32)
        d["tabCq"], d["tabSq"], d["tabCk"], d["tabSk"] = _rot_tables(t0)
        d["maskA"] = mask_a if half == 0 else sh["maskB"]
        per_core.append(d)
    return sh, per_core



ENGINES = ("sync", "scalar", "vector", "tensor", "gpsimd")


class Buf:
    __slots__ = ("name", "w", "r")

    def __init__(self, name):
        self.name = name
        self.w = None
        self.r = []


def ts(i, n=P):
    return slice(i * n, (i + 1) * n)


def npieces(width, maxn=512):
    out, o = [], 0
    while o < width:
        w = min(maxn, width - o)
        out.append((o, w))
        o += w
    return out


class Prog:
    def __init__(self, taps=(), upto=None):
        self.upto = upto
        self.nc = bass.Bass("TRN2", target_bir_lowering=False)
        self.stream = {e: [] for e in ENGINES}
        self.sems = {}
        self.cnt = {}
        self.waited = {e: {} for e in ENGINES}
        self._ctx = []
        self.dmaidx = {}
        self.DT = {}
        self.SB = {}
        self.PS = {}
        self.BF = {}
        self.taps = set(taps)
        self.nb = 0

    def ctx(self, cm):
        v = cm.__enter__()
        self._ctx.append(cm)
        return v

    def scope(self):
        return len(self._ctx)

    def close_scope(self, mark):
        while len(self._ctx) > mark:
            self._ctx.pop().__exit__(None, None, None)

    def buf(self, name=None):
        self.nb += 1
        return Buf(name or f"b{self.nb}")

    def sem(self, name):
        if name not in self.sems:
            self.sems[name] = self.ctx(self.nc.semaphore(name))
            self.cnt[name] = 0
        return self.sems[name]

    def op(self, eng, fn, reads=(), writes=(), dma=False):
        extra_wait = None
        if dma:
            idx = self.dmaidx.get(eng, 0)
            self.dmaidx[eng] = idx + 1
            slot = idx % NDMASLOT
            semname, amt, tag = f"dm_{eng}{slot}", 16, "dma"
            self.sem(semname)
            prev = self.cnt[semname]
            if prev > 0:
                extra_wait = (semname, prev)
        else:
            semname, amt, tag = "tk_" + eng, 1, eng
        self.sem(semname)
        want = {}

        def add(tok):
            if tok is None:
                return
            ttag, tsem, tval = tok
            if want.get(tsem, 0) < tval:
                want[tsem] = tval
        for b in reads:
            add(b.w)
        for b in writes:
            add(b.w)
            for t in b.r:
                add(t)
        if extra_wait is not None:
            s, v = extra_wait
            if want.get(s, 0) < v:
                want[s] = v
        wd = self.waited[eng]
        waits = []
        for s, v in want.items():
            if wd.get(s, 0) < v:
                waits.append((s, v))
                wd[s] = v
        self.cnt[semname] += amt
        val = self.cnt[semname]
        sems = self.sems

        def run(e, fn=fn, waits=waits, semname=semname, amt=amt):
            for s, v in waits:
                e.wait_ge(sems[s], v)
            inst = fn(e)
            assert inst is not None, f"op on {semname} returned None"
            inst.then_inc(sems[semname], amt)

        self.stream[eng].append(run)
        tok = (tag, semname, val)
        for b in reads:
            b.r.append(tok)
        for b in writes:
            b.w = tok
            b.r = []
        return tok

    def raw(self, eng, fn):
        self.stream[eng].append(fn)

    def barrier(self):
        targets = [(s, v) for s, v in self.cnt.items() if v > 0]
        for e in ENGINES:
            wd = self.waited[e]
            todo = [(s, v) for (s, v) in targets if wd.get(s, 0) < v]
            for s, v in todo:
                wd[s] = v
            sems = self.sems

            def wt(x, todo=todo):
                for s, v in todo:
                    x.wait_ge(sems[s], v)
            self.raw(e, wt)

    def din(self, name, shape, dtype=F32):
        self.DT[name] = self.nc.dram_tensor(name, list(shape), dtype, kind="ExternalInput").ap()

    def dout(self, name, shape, dtype=F32):
        self.DT[name] = self.nc.dram_tensor(name, list(shape), dtype, kind="ExternalOutput").ap()

    def dint(self, name, shape, dtype=F32):
        self.DT[name] = self.nc.dram_tensor(name, list(shape), dtype, kind="Internal").ap()

    def sb(self, name, shape, dtype=F32, nbufs=1):
        self.nb += 1
        self.SB[name] = self.ctx(self.nc.sbuf_tensor(f"sb{self.nb}_{name}", list(shape), dtype))
        if nbufs == 1:
            self.BF[name] = self.buf(name)
            return self.BF[name]
        bl = [self.buf(f"{name}:{i}") for i in range(nbufs)]
        self.BF[name] = bl
        return bl

    def psum(self, name, shape, dtype=F32):
        self.PS[name] = self.ctx(self.nc.psum_tensor("ps_" + name, list(shape), dtype))
        self.BF[name] = self.buf(name)

    def mmg(self, psname, emits, reads):
        def fn(e):
            inst = None
            for em in emits:
                inst = em(e)
            return inst
        return self.op("tensor", fn, reads=reads, writes=[self.BF[psname]])

    def dma(self, eng, out_ap, in_ap, reads=(), writes=()):
        return self.op(eng, lambda e: e.dma_start(out=out_ap, in_=in_ap),
                       reads=reads, writes=writes, dma=True)

    def tap(self, name, ap, rbufs):
        if name in self.taps:
            self.dma("sync", self.DT["tap_" + name], ap, reads=rbufs)

    def bcast(self, psname, row_ap, row_buf, width):
        SB, PS = self.SB, self.PS
        scrB = self.BF["bc_t"]
        self.op("vector", lambda e: e.tensor_copy(
            SB["bc_hi"][:1, 0:width], row_ap), reads=[row_buf], writes=[scrB])
        self.op("vector", lambda e: e.tensor_sub(
            SB["bc_t"][:1, 0:width], row_ap, SB["bc_hi"][:1, 0:width]),
            reads=[row_buf, scrB], writes=[scrB])
        self.op("vector", lambda e: e.tensor_copy(
            SB["bc_lo"][:1, 0:width], SB["bc_t"][:1, 0:width]), reads=[scrB], writes=[scrB])
        emits = []
        for o, w in npieces(width):
            emits.append(lambda e, o=o, w=w: e.matmul(
                PS[psname][:, o:o + w], SB["onescol"][:1, :], SB["bc_hi"][:1, o:o + w],
                start=True, stop=False))
            emits.append(lambda e, o=o, w=w: e.matmul(
                PS[psname][:, o:o + w], SB["onescol"][:1, :], SB["bc_lo"][:1, o:o + w],
                start=False, stop=True))
        return self.mmg(psname, emits, [scrB, self.BF["onescol"]])

    def build(self):
        nc, DT, SB, PS, BF = self.nc, self.DT, self.SB, self.PS, self.BF
        self.din("x", (1024, WG))
        self.din("selmask", (P, 1))
        for n in ("winT", "wgateT", "mgateT", "mupT"):
            self.din(n, (1024, 1536))
        for n in ("rginT", "rggateT"):
            self.din(n, (1536, 1536))
        for n in ("woutT", "mdownT"):
            self.din(n, (1536, 1024))
        self.din("convw", (P, NHT * 4)); self.din("convb", (P, NHT))
        self.din("rgib", (P, NHT)); self.din("rggb", (P, NHT)); self.din("flam", (P, NHT))
        self.din("n1w", (P, NFT)); self.din("n2w", (P, NFT))
        for l in range(2):
            self.din(f"wqkv{l}", (1024, 1536)); self.din(f"wo{l}", (512, 1024))
            self.din(f"w1p{l}", (1024, 5632)); self.din(f"b1p{l}", (P, 44))
            self.din(f"w2p{l}", (2816, 1024)); self.din(f"b2{l}", (P, NFT))
            self.din(f"lng{l}", (P, NFT)); self.din(f"lnb{l}", (P, NFT))
            for n in ("qs", "qssw", "ks", "kssw"):
                self.din(f"{n}{l}", (P, 1))
        self.din("ident", (P, P)); self.din("perm", (P, P))
        self.din("onescol", (1, P)); self.din("onesk", (P, 1)); self.din("epsc", (P, 1))
        self.din("pat2", (2, P)); self.din("blockones", (P, 2))
        self.din("maskA", (4, P, 640)); self.din("maskB", (4, P, 640))
        for n in ("tabCq", "tabSq", "tabCk", "tabSk"):
            self.din(n, (P, T))
        self.dout("y", (1024, T))
        self.dint("x1scr", (1024, T))
        self.dint("gascr", (NHT * P, T), BF16)
        self.dint("cin_carry", (P, NHT)); self.dint("cout_carry", (2, P, NHT))
        for l in range(2):
            self.dint(f"cin_kv{l}", (8, P, 512), BF16)
            self.dint(f"cout_kv{l}", (2, 8, P, 512), BF16)
            self.dint(f"cin_hc{l}", (NFT, P)); self.dint(f"cout_hc{l}", (2, NFT, P))
            self.dint(f"cin_pa{l}", (NFT, P)); self.dint(f"cout_pa{l}", (2, NFT, P))
        tapshapes = {
            "g_gb": ((P, NFT, TC + 3), BF16), "g_uc": ((P, NHT, TC), F32),
            "g_hlast": ((P, NHT), F32), "g_prod": ((P, NHT, T), BF16),
            "g_A": ((P, NFT, GP), F32), "x1": ((P, NFT, GP), F32),
            "f_h1": ((P, NFT, 1 + T), F32),
            "f_q": ((P, 4, T), BF16), "f_k": ((P, 4, 2560), BF16),
            "f_v": ((P, 20, 512), BF16), "f_o": ((P, 4, 512), BF16),
            "f_pa": ((P, NFT, 1 + T), F32), "f_out": ((P, NFT, 1 + T), F32),
        }
        for tname in self.taps:
            shp, dt = tapshapes[tname]
            self.dout("tap_" + tname, shp, dt)

        for n in ("pa", "pb", "pc"):
            self.psum(n, (P, 512))
        self.psum("ps0", (P, 640)); self.psum("ps1", (P, 640))
        self.psum("pt", (P, 512), BF16)
        self.BF["pt0"] = self.buf("pt0"); self.BF["pt1"] = self.buf("pt1")
        self.BF["pt2"] = self.buf("pt2"); self.BF["pt3"] = self.buf("pt3")

        for n, shp, dt in (("ident", (P, P), BF16), ("perm", (P, P), BF16),
                           ("onescol", (1, P), BF16), ("onesk", (P, 1), BF16),
                           ("pat2", (2, P), BF16), ("blockones", (P, 2), BF16),
                           ("selmask", (P, 1), F32), ("convw", (P, NHT * 4), F32),
                           ("convb", (P, NHT), F32), ("rgib", (P, NHT), F32),
                           ("rggb", (P, NHT), F32), ("negsp8", (P, NHT), F32),
                           ("n1w", (P, NFT), F32), ("n2w", (P, NFT), F32),
                           ("flam", (P, NHT), F32), ("spt", (P, NHT), F32),
                           ("hlast", (P, NHT), F32), ("alast", (P, NHT), F32),
                           ("carry", (P, NHT), F32), ("epsc", (P, 1), F32),
                           ("bc_t", (1, 640), F32), ("bc_hi", (1, 640), BF16),
                           ("bc_lo", (1, 640), BF16)):
            self.sb(n, shp, dt)
        smalls2 = []
        for l in range(2):
            for n in (f"qs{l}", f"qssw{l}", f"ks{l}", f"kssw{l}"):
                self.sb(n, (P, 1), F32)
                smalls2.append(n)
            for n in (f"b1p{l}",):
                self.sb(n, (P, 44), F32)
                smalls2.append(n)
            for n in (f"b2{l}", f"lng{l}", f"lnb{l}"):
                self.sb(n, (P, NFT), F32)
                smalls2.append(n)

        for n in ("ident", "perm", "onescol", "onesk", "pat2", "blockones"):
            self.dma("gpsimd", SB[n][:, :] if n != "onescol" else SB[n][:1, :],
                     DT[n][:, :] if n != "onescol" else DT[n][:1, :], writes=[BF[n]])
        for n in ("selmask", "convw", "convb", "rgib", "rggb", "n1w", "n2w", "flam", "epsc"):
            self.dma("sync", SB[n][:, :], DT[n][:, :], writes=[BF[n]])
        for n in smalls2:
            self.dma("sync", SB[n][:, :], DT[n][:, :], writes=[BF[n]])
        ops = self.op
        ops("scalar", lambda e: e.activation(out=SB["spt"][:, :], in_=SB["flam"][:, :],
                                             func=AF.Abs), reads=[BF["flam"]], writes=[BF["spt"]])
        ops("scalar", lambda e: e.activation(out=SB["spt"][:, :], in_=SB["spt"][:, :],
                                             func=AF.Exp, scale=-1.0), reads=[BF["spt"]], writes=[BF["spt"]])
        ops("scalar", lambda e: e.activation(out=SB["spt"][:, :], in_=SB["spt"][:, :],
                                             func=AF.Ln, bias=1.0), reads=[BF["spt"]], writes=[BF["spt"]])
        ops("vector", lambda e: e.tensor_scalar_max(SB["negsp8"][:, :], SB["flam"][:, :], 0.0),
            reads=[BF["flam"]], writes=[BF["negsp8"]])
        ops("vector", lambda e: e.tensor_add(SB["negsp8"][:, :], SB["negsp8"][:, :], SB["spt"][:, :]),
            reads=[BF["negsp8"], BF["spt"]], writes=[BF["negsp8"]])
        ops("vector", lambda e: e.tensor_scalar_mul(SB["negsp8"][:, :], SB["negsp8"][:, :], -8.0),
            reads=[BF["negsp8"]], writes=[BF["negsp8"]])
        ops("vector", lambda e: e.memset(SB["hlast"][:, :], 0.0), writes=[BF["hlast"]])
        ops("vector", lambda e: e.memset(SB["alast"][:, :], 1.0), writes=[BF["alast"]])

        if self.upto != "consts":
            self.phaseG()
            if self.upto is None:
                self.phaseF()

        self.barrier()
        blk = self.ctx(nc.Block())
        for eng in ENGINES:
            def body(e, eng=eng):
                for fn in self.stream[eng]:
                    fn(e)
            getattr(blk, eng)(body)
        self.close_scope(0)
        return nc

    class WStrip:
        def __init__(self, prog, name, nk, mc, nbuf=2):
            self.p = prog
            self.nk, self.mc, self.nbuf = nk, mc, nbuf
            prog.sb(name, (P, nbuf, nk, mc), BF16)
            self.t = prog.SB[name]
            self.bufs = [prog.buf(f"{name}:{i}") for i in range(nbuf)]
            self.i = 0
            prog.op("vector", lambda e: e.memset(self.t[:, :, :, :], 0.0),
                    writes=self.bufs)

        def load(self, dram, c0, nk=None, mc=None):
            nk = nk or self.nk
            mc = mc or self.mc
            sl = self.i % self.nbuf
            self.p.dma("gpsimd", self.t[:, sl, 0:nk, 0:mc],
                       self.p.DT[dram][0:nk * P, c0:c0 + mc].rearrange("(k p) m -> p k m", p=P),
                       writes=[self.bufs[sl]])
            self.i += 1
            return sl

        def ap(self, sl, k, mcs=None):
            return self.t[:, sl, k, mcs] if mcs is not None else self.t[:, sl, k, :]

        def b(self, sl):
            return self.bufs[sl]

    def phaseG(self):
        SB, PS, BF, DT = self.SB, self.PS, self.BF, self.DT
        op, mmg, dma, bcast = self.op, self.mmg, self.dma, self.bcast
        gall = self.scope()
        self.sb("mu_row", (1, WG)); self.sb("r_row", (1, WG))
        rowsB = BF["mu_row"]
        BF["r_row"] = rowsB
        prodB = self.sb("g_prod", (P, NHT, T), BF16, nbufs=NHT)

        g1 = self.scope()
        ucbB = self.sb("g_ucb", (P, NHT, TC), BF16, nbufs=NHT)
        uc32B = self.sb("g_uc32", (P, NHT, TC), F32, nbufs=NHT)
        gbufB = self.sb("g_gbuf", (P, NFT, TC + 3), BF16, nbufs=NFT)
        xtB = self.sb("g_xt", (P, NFT, TC + 3), F32, nbufs=NFT)
        xbB = self.sb("g_xb", (P, 2, TC + 3), BF16, nbufs=2)
        x2bB = self.sb("g_x2b", (P, 2, TC + 3), BF16, nbufs=2)
        for n in ("g_wk1", "g_wk2", "g_wk3", "g_wk4", "g_wk5"):
            self.sb(n, (P, 2, TC), F32, nbufs=2)
        self.sb("g_gax", (P, 2, TC), BF16, nbufs=2)
        wk1B, wk2B, wk3B, wk4B, wk5B = (BF["g_wk1"], BF["g_wk2"], BF["g_wk3"],
                                        BF["g_wk4"], BF["g_wk5"])
        gaxB = BF["g_gax"]
        self.sb("g_rba", (1, TC + 3), BF16); self.sb("g_rbb", (1, TC + 3), BF16)
        self.sb("g_rowa", (1, TC + 3)); self.sb("g_rowb", (1, TC + 3)); self.sb("g_rowc", (1, TC + 3))
        growB = BF["g_rowa"]
        for n in ("g_rowb", "g_rowc", "g_rba", "g_rbb"):
            BF[n] = growB
        stWin = Prog.WStrip(self, "sWin", 8, P)
        stWgt = Prog.WStrip(self, "sWgt", 8, P)
        stRgi = Prog.WStrip(self, "sRgi", NHT, P)
        stRgg = Prog.WStrip(self, "sRgg", NHT, P)

        for c in range(GC):
            xc0 = c * TC
            cw = TC + 3
            for ft in range(NFT):
                dma("sync", SB["g_xt"][:, ft, :], DT["x"][ts(ft), xc0:xc0 + cw],
                    writes=[xtB[ft]])
            for ft in range(NFT):
                sl = ft % 2
                op("vector", lambda e, ft=ft, sl=sl: e.tensor_copy(
                    SB["g_xb"][:, sl, :], SB["g_xt"][:, ft, :]),
                    reads=[xtB[ft]], writes=[xbB[sl]])
                op("scalar", lambda e, ft=ft, sl=sl: e.activation(
                    out=SB["g_x2b"][:, sl, :], in_=SB["g_xt"][:, ft, :], func=AF.Square),
                    reads=[xtB[ft]], writes=[x2bB[sl]])
                mmg("ps0", [
                    (lambda e, o=o, w=w, sl=sl, ft=ft: e.matmul(
                        PS["ps0"][0:1, o:o + w], SB["onesk"][:, :1],
                        SB["g_xb"][:, sl, o:o + w], start=ft == 0, stop=ft == NFT - 1))
                    for o, w in npieces(cw)], [xbB[sl], BF["onesk"]])
                mmg("ps1", [
                    (lambda e, o=o, w=w, sl=sl, ft=ft: e.matmul(
                        PS["ps1"][0:1, o:o + w], SB["onesk"][:, :1],
                        SB["g_x2b"][:, sl, o:o + w], start=ft == 0, stop=ft == NFT - 1))
                    for o, w in npieces(cw)], [x2bB[sl], BF["onesk"]])
            op("scalar", lambda e, xc0=xc0, cw=cw: e.activation(
                out=SB["mu_row"][:1, xc0:xc0 + cw], in_=PS["ps0"][0:1, 0:cw],
                func=AF.Copy, scale=1.0 / 1024), reads=[BF["ps0"]], writes=[rowsB])
            op("scalar", lambda e, cw=cw: e.activation(
                out=SB["g_rowb"][:1, 0:cw], in_=PS["ps1"][0:1, 0:cw],
                func=AF.Copy, scale=1.0 / 1024), reads=[BF["ps1"]], writes=[growB])
            op("vector", lambda e, xc0=xc0, cw=cw: e.tensor_tensor(
                SB["g_rowa"][:1, 0:cw], SB["mu_row"][:1, xc0:xc0 + cw],
                SB["mu_row"][:1, xc0:xc0 + cw], op=ALU.mult), reads=[rowsB], writes=[growB])
            op("vector", lambda e, cw=cw: e.tensor_sub(
                SB["g_rowb"][:1, 0:cw], SB["g_rowb"][:1, 0:cw], SB["g_rowa"][:1, 0:cw]),
                reads=[growB], writes=[growB])
            op("scalar", lambda e, cw=cw: e.activation(
                out=SB["g_rowc"][:1, 0:cw], in_=SB["g_rowb"][:1, 0:cw],
                func=AF.Sqrt, bias=SB["epsc"][0:1, 0:1]), reads=[growB, BF["epsc"]], writes=[growB])
            op("vector", lambda e, xc0=xc0, cw=cw: e.reciprocal(
                SB["r_row"][:1, xc0:xc0 + cw], SB["g_rowc"][:1, 0:cw]),
                reads=[growB], writes=[rowsB])
            op("vector", lambda e, xc0=xc0, cw=cw: e.tensor_tensor(
                SB["g_rowa"][:1, 0:cw], SB["g_rowb"][:1, 0:cw],
                SB["r_row"][:1, xc0:xc0 + cw], op=ALU.mult), reads=[growB, rowsB], writes=[growB])
            op("vector", lambda e, xc0=xc0, cw=cw: e.tensor_tensor(
                SB["g_rowa"][:1, 0:cw], SB["g_rowa"][:1, 0:cw],
                SB["r_row"][:1, xc0:xc0 + cw], op=ALU.mult), reads=[growB, rowsB], writes=[growB])
            op("scalar", lambda e, cw=cw: e.activation(
                out=SB["g_rowa"][:1, 0:cw], in_=SB["g_rowa"][:1, 0:cw],
                func=AF.Sqrt, bias=SB["epsc"][0:1, 0:1]), reads=[growB, BF["epsc"]], writes=[growB])
            op("vector", lambda e, cw=cw: e.reciprocal(
                SB["g_rowc"][:1, 0:cw], SB["g_rowa"][:1, 0:cw]),
                reads=[growB], writes=[growB])
            op("vector", lambda e, xc0=xc0, cw=cw: e.tensor_tensor(
                SB["g_rowc"][:1, 0:cw], SB["g_rowc"][:1, 0:cw],
                SB["r_row"][:1, xc0:xc0 + cw], op=ALU.mult), reads=[growB, rowsB], writes=[growB])
            self.bcast("ps0", SB["mu_row"][:1, xc0:xc0 + cw], rowsB, cw)
            self.bcast("ps1", SB["g_rowc"][:1, 0:cw], growB, cw)
            for ft in range(NFT):
                op("vector", lambda e, ft=ft, cw=cw: e.tensor_sub(
                    SB["g_xt"][:, ft, :], SB["g_xt"][:, ft, :], PS["ps0"][:, 0:cw]),
                    reads=[xtB[ft], BF["ps0"]], writes=[xtB[ft]])
                op("vector", lambda e, ft=ft, cw=cw: e.scalar_tensor_tensor(
                    SB["g_gbuf"][:, ft, :], SB["g_xt"][:, ft, :], SB["n1w"][:, ft:ft + 1],
                    PS["ps1"][:, 0:cw], op0=ALU.mult, op1=ALU.mult),
                    reads=[xtB[ft], BF["n1w"], BF["ps1"]], writes=[gbufB[ft]])
            if c == 0:
                self.tap("g_gb", SB["g_gbuf"][:, :, :], gbufB)

            if self.upto == "Gstats":
                continue
            for m in range(NHT):
                sl = stWin.load("winT", m * P)
                psn = "ps0" if m % 2 == 0 else "ps1"
                mmg(psn, [
                    (lambda e, o=o, w=w, k=k, sl=sl, psn=psn: e.matmul(
                        PS[psn][:, o:o + w], stWin.ap(sl, k),
                        SB["g_gbuf"][:, k, o:o + w], start=k == 0, stop=k == 7))
                    for o, w in npieces(cw) for k in range(8)],
                    [stWin.b(sl)] + gbufB)
                mq = m % 2
                op("vector", lambda e, psn=psn, m=m, mq=mq: e.tensor_scalar(
                    SB["g_wk1"][:, mq, :], PS[psn][:, 0:TC], SB["convw"][:, 4 * m:4 * m + 1],
                    None, op0=ALU.mult), reads=[BF[psn], BF["convw"]], writes=[wk1B[mq]])
                op("vector", lambda e, psn=psn, m=m, mq=mq: e.scalar_tensor_tensor(
                    SB["g_wk2"][:, mq, :], PS[psn][:, 1:TC + 1], SB["convw"][:, 4 * m + 1:4 * m + 2],
                    SB["g_wk1"][:, mq, :], op0=ALU.mult, op1=ALU.add),
                    reads=[BF[psn], BF["convw"], wk1B[mq]], writes=[wk2B[mq]])
                op("vector", lambda e, psn=psn, m=m, mq=mq: e.scalar_tensor_tensor(
                    SB["g_wk1"][:, mq, :], PS[psn][:, 2:TC + 2], SB["convw"][:, 4 * m + 2:4 * m + 3],
                    SB["g_wk2"][:, mq, :], op0=ALU.mult, op1=ALU.add),
                    reads=[BF[psn], BF["convw"], wk2B[mq]], writes=[wk1B[mq]])
                op("vector", lambda e, psn=psn, m=m, mq=mq: e.scalar_tensor_tensor(
                    SB["g_wk2"][:, mq, :], PS[psn][:, 3:TC + 3], SB["convw"][:, 4 * m + 3:4 * m + 4],
                    SB["g_wk1"][:, mq, :], op0=ALU.mult, op1=ALU.add),
                    reads=[BF[psn], BF["convw"], wk1B[mq]], writes=[wk2B[mq]])
                op("vector", lambda e, m=m, mq=mq: e.tensor_scalar(
                    SB["g_uc32"][:, m, :], SB["g_wk2"][:, mq, :], SB["convb"][:, m:m + 1],
                    None, op0=ALU.add),
                    reads=[wk2B[mq], BF["convb"]], writes=[uc32B[m]])
                op("vector", lambda e, m=m: e.tensor_copy(
                    SB["g_ucb"][:, m, :], SB["g_uc32"][:, m, :]),
                    reads=[uc32B[m]], writes=[ucbB[m]])
            if c == 0:
                self.tap("g_uc", SB["g_uc32"][:, :, :], uc32B)

            if self.upto == "Gu":
                continue
            for m in range(NHT):
                mq = m % 2
                igb = "pa" if mq == 0 else "ps0"
                rgb = "pb" if mq == 0 else "ps1"
                sli = stRgi.load("rginT", m * P)
                slg = stRgg.load("rggateT", m * P)
                mmg(igb, [
                    (lambda e, k=k, sli=sli, igb=igb: e.matmul(
                        PS[igb][:, 0:TC], stRgi.ap(sli, k), SB["g_ucb"][:, k, :],
                        start=k == 0, stop=k == NHT - 1))
                    for k in range(NHT)], [stRgi.b(sli)] + ucbB)
                mmg(rgb, [
                    (lambda e, k=k, slg=slg, rgb=rgb: e.matmul(
                        PS[rgb][:, 0:TC], stRgg.ap(slg, k), SB["g_ucb"][:, k, :],
                        start=k == 0, stop=k == NHT - 1))
                    for k in range(NHT)], [stRgg.b(slg)] + ucbB)
                op("scalar", lambda e, m=m, mq=mq, rgb=rgb: e.activation(
                    out=SB["g_wk1"][:, mq, :], in_=PS[rgb][:, 0:TC], func=AF.Sigmoid,
                    bias=SB["rggb"][:, m:m + 1]), reads=[BF[rgb], BF["rggb"]], writes=[wk1B[mq]])
                op("scalar", lambda e, m=m, mq=mq: e.activation(
                    out=SB["g_wk2"][:, mq, :], in_=SB["g_wk1"][:, mq, :], func=AF.Exp,
                    scale=SB["negsp8"][:, m:m + 1]),
                    reads=[wk1B[mq], BF["negsp8"]], writes=[wk2B[mq]])
                op("scalar", lambda e, m=m, mq=mq, igb=igb: e.activation(
                    out=SB["g_wk3"][:, mq, :], in_=PS[igb][:, 0:TC], func=AF.Sigmoid,
                    bias=SB["rgib"][:, m:m + 1]), reads=[BF[igb], BF["rgib"]], writes=[wk3B[mq]])
                op("vector", lambda e, mq=mq: e.tensor_mul(
                    SB["g_wk4"][:, mq, :], SB["g_wk2"][:, mq, :], SB["g_wk2"][:, mq, :]),
                    reads=[wk2B[mq]], writes=[wk4B[mq]])
                op("vector", lambda e, mq=mq: e.tensor_scalar(
                    SB["g_wk4"][:, mq, :], SB["g_wk4"][:, mq, :], -1.0, 1.0 + 1e-6,
                    op0=ALU.mult, op1=ALU.add), reads=[wk4B[mq]], writes=[wk4B[mq]])
                op("scalar", lambda e, mq=mq: e.activation(
                    out=SB["g_wk4"][:, mq, :], in_=SB["g_wk4"][:, mq, :], func=AF.Sqrt),
                    reads=[wk4B[mq]], writes=[wk4B[mq]])
                op("vector", lambda e, mq=mq: e.tensor_mul(
                    SB["g_wk3"][:, mq, :], SB["g_wk3"][:, mq, :], SB["g_wk4"][:, mq, :]),
                    reads=[wk3B[mq], wk4B[mq]], writes=[wk3B[mq]])
                op("vector", lambda e, m=m, mq=mq: e.tensor_mul(
                    SB["g_wk3"][:, mq, :], SB["g_wk3"][:, mq, :], SB["g_uc32"][:, m, :]),
                    reads=[wk3B[mq], uc32B[m]], writes=[wk3B[mq]])
                op("vector", lambda e, m=m, mq=mq: e.tensor_tensor_scan(
                    SB["g_wk5"][:, mq, :], SB["g_wk2"][:, mq, :], SB["g_wk3"][:, mq, :],
                    SB["hlast"][:, m:m + 1], op0=ALU.mult, op1=ALU.add),
                    reads=[wk2B[mq], wk3B[mq], BF["hlast"]], writes=[wk5B[mq]])
                op("vector", lambda e, m=m, mq=mq: e.tensor_copy(
                    SB["hlast"][:, m:m + 1], SB["g_wk5"][:, mq, TC - 1:TC]),
                    reads=[wk5B[mq]], writes=[BF["hlast"]])
                op("vector", lambda e, m=m, mq=mq: e.tensor_tensor_scan(
                    SB["g_wk4"][:, mq, :], SB["g_wk2"][:, mq, :], SB["g_wk2"][:, mq, :],
                    SB["alast"][:, m:m + 1], op0=ALU.mult, op1=ALU.bypass),
                    reads=[wk2B[mq], BF["alast"]], writes=[wk4B[mq]])
                op("vector", lambda e, m=m, mq=mq: e.tensor_copy(
                    SB["alast"][:, m:m + 1], SB["g_wk4"][:, mq, TC - 1:TC]),
                    reads=[wk4B[mq]], writes=[BF["alast"]])
                slw = stWgt.load("wgateT", m * P)
                gtb = "pc"
                mmg(gtb, [
                    (lambda e, k=k, slw=slw, gtb=gtb: e.matmul(
                        PS[gtb][:, 0:TC], stWgt.ap(slw, k), SB["g_gbuf"][:, k, 3:3 + TC],
                        start=k == 0, stop=k == 7))
                    for k in range(8)], [stWgt.b(slw)] + gbufB)
                op("scalar", lambda e, mq=mq, gtb=gtb: e.activation(
                    out=SB["g_wk1"][:, mq, :], in_=PS[gtb][:, 0:TC], func=AF.Gelu),
                    reads=[BF[gtb]], writes=[wk1B[mq]])
                op("vector", lambda e, m=m, mq=mq, c=c: e.tensor_mul(
                    SB["g_prod"][:, m, c * TC:(c + 1) * TC], SB["g_wk1"][:, mq, :],
                    SB["g_wk5"][:, mq, :]),
                    reads=[wk1B[mq], wk5B[mq]], writes=[prodB[m]])
                op("vector", lambda e, mq=mq: e.tensor_mul(
                    SB["g_gax"][:, mq, :], SB["g_wk1"][:, mq, :], SB["g_wk4"][:, mq, :]),
                    reads=[wk1B[mq], wk4B[mq]], writes=[gaxB[mq]])
                dma("sync", DT["gascr"][ts(m), c * TC:(c + 1) * TC], SB["g_gax"][:, mq, :],
                    reads=[gaxB[mq]])

        if self.upto in ("Gstats", "Gu", "Grg"):
            self.barrier()
            self.close_scope(g1)
            self.close_scope(gall)
            return
        ccB = self.buf("cc")
        dma("sync", DT["cin_carry"][:, :], SB["hlast"][:, :], reads=[BF["hlast"]], writes=[ccB])
        op("gpsimd", lambda e: e.collective_compute(
            "AllGather", ALU.bypass, replica_groups=RG,
            ins=[DT["cin_carry"].opt()], outs=[DT["cout_carry"].opt()]),
            reads=[ccB], writes=[ccB])
        dma("sync", SB["carry"][:, :], DT["cout_carry"][0, :, :], reads=[ccB], writes=[BF["carry"]])
        op("vector", lambda e: e.tensor_scalar(
            SB["carry"][:, :], SB["carry"][:, :], SB["selmask"][:, 0:1], None, op0=ALU.mult),
            reads=[BF["carry"], BF["selmask"]], writes=[BF["carry"]])
        self.tap("g_hlast", SB["hlast"][:, :], [BF["hlast"]])

        self.barrier()
        self.close_scope(g1)
        if self.upto == "G1":
            self.close_scope(gall)
            return

        g3 = self.scope()
        garB = self.sb("g_gar", (P, 2, T), BF16, nbufs=2)
        xgB = self.sb("g_xg", (P, NFT, GP), F32, nbufs=NFT)
        xpB = self.sb("g_xp", (P, NFT, GP), F32, nbufs=NFT)
        n2bB = self.sb("g_n2b", (P, NFT, GP), BF16, nbufs=NFT)
        pmB = self.sb("g_pm", (P, NHT, GP), BF16, nbufs=NHT)
        ggB = self.sb("g_gg", (P, 2, GP), F32, nbufs=2)
        sqB = self.sb("g_sq", (P, 2, GP), BF16, nbufs=2)
        abB = self.sb("g_ab", (P, 2, GP), BF16, nbufs=2)
        self.sb("g_rb2", (1, GP), BF16); self.sb("g_rb3", (1, GP), BF16)
        self.sb("g_row1", (1, GP)); self.sb("g_row2", (1, GP)); self.sb("g_row3", (1, GP))
        g3rowB = BF["g_row1"]
        for n in ("g_row2", "g_row3", "g_rb2", "g_rb3"):
            BF[n] = g3rowB
        stWo = Prog.WStrip(self, "sWo", NHT, P)
        stMg = Prog.WStrip(self, "sMg", 8, P)
        stMu = Prog.WStrip(self, "sMu", 8, P)
        stMd = Prog.WStrip(self, "sMd", NHT, P)

        for m in range(NHT):
            sl = m % 2
            dma("sync", SB["g_gar"][:, sl, :], DT["gascr"][ts(m), :], writes=[garB[sl]])
            op("vector", lambda e, m=m, sl=sl: e.scalar_tensor_tensor(
                SB["g_prod"][:, m, :], SB["g_gar"][:, sl, :], SB["carry"][:, m:m + 1],
                SB["g_prod"][:, m, :], op0=ALU.mult, op1=ALU.add),
                reads=[garB[sl], BF["carry"], prodB[m]], writes=[prodB[m]])
        self.tap("g_prod", SB["g_prod"][:, :, :], prodB)

        for pp in range(NGP):
            p0 = pp * GP
            bcast("ps0", SB["mu_row"][:1, 3 + p0:3 + p0 + GP], rowsB, GP)
            bcast("ps1", SB["r_row"][:1, 3 + p0:3 + p0 + GP], rowsB, GP)
            for ft in range(NFT):
                fq = ft % 2
                mxb = "pa" if fq == 0 else "pb"
                slw = stWo.load("woutT", ft * P)
                mmg(mxb, [
                    (lambda e, k=k, slw=slw, p0=p0, mxb=mxb: e.matmul(
                        PS[mxb][:, :], stWo.ap(slw, k), SB["g_prod"][:, k, p0:p0 + GP],
                        start=k == 0, stop=k == NHT - 1))
                    for k in range(NHT)], [stWo.b(slw)] + prodB)
                dma("sync", SB["g_xp"][:, ft, :], DT["x"][ts(ft), p0 + 3:p0 + 3 + GP],
                    writes=[xpB[ft]])
                op("vector", lambda e, ft=ft: e.tensor_sub(
                    SB["g_xg"][:, ft, :], SB["g_xp"][:, ft, :], PS["ps0"][:, 0:GP]),
                    reads=[xpB[ft], BF["ps0"]], writes=[xgB[ft]])
                op("vector", lambda e, ft=ft: e.tensor_mul(
                    SB["g_xg"][:, ft, :], SB["g_xg"][:, ft, :], PS["ps1"][:, 0:GP]),
                    reads=[xgB[ft], BF["ps1"]], writes=[xgB[ft]])
                op("vector", lambda e, ft=ft, mxb=mxb: e.tensor_add(
                    SB["g_xg"][:, ft, :], SB["g_xg"][:, ft, :], PS[mxb][:, :]),
                    reads=[xgB[ft], BF[mxb]], writes=[xgB[ft]])
            for ft in range(NFT):
                fq = ft % 2
                op("scalar", lambda e, ft=ft, fq=fq: e.activation(
                    out=SB["g_sq"][:, fq, :], in_=SB["g_xg"][:, ft, :], func=AF.Square),
                    reads=[xgB[ft]], writes=[sqB[fq]])
                mmg("pc", [
                    (lambda e, ft=ft, fq=fq: e.matmul(
                        PS["pc"][0:1, :], SB["onesk"][:, :1], SB["g_sq"][:, fq, :],
                        start=ft == 0, stop=ft == NFT - 1))], [sqB[fq], BF["onesk"]])
            op("scalar", lambda e: e.activation(
                out=SB["g_row1"][:1, :], in_=PS["pc"][0:1, :], func=AF.Copy,
                scale=1.0 / 1024), reads=[BF["pc"]], writes=[g3rowB])
            op("scalar", lambda e: e.activation(
                out=SB["g_row1"][:1, :], in_=SB["g_row1"][:1, :], func=AF.Sqrt,
                bias=SB["epsc"][0:1, 0:1]), reads=[g3rowB, BF["epsc"]], writes=[g3rowB])
            op("vector", lambda e: e.reciprocal(
                SB["g_row2"][:1, :], SB["g_row1"][:1, :]), reads=[g3rowB], writes=[g3rowB])
            bcast("ps1", SB["g_row2"][:1, 0:GP], g3rowB, GP)
            for ft in range(NFT):
                op("vector", lambda e, ft=ft: e.scalar_tensor_tensor(
                    SB["g_n2b"][:, ft, :], SB["g_xg"][:, ft, :], SB["n2w"][:, ft:ft + 1],
                    PS["ps1"][:, 0:GP], op0=ALU.mult, op1=ALU.mult),
                    reads=[xgB[ft], BF["n2w"], BF["ps1"]], writes=[n2bB[ft]])
            for m in range(NHT):
                mq = m % 2
                gb_ = "pa" if mq == 0 else "pc"
                ub_ = "pb" if mq == 0 else "ps0"
                slg = stMg.load("mgateT", m * P)
                slu = stMu.load("mupT", m * P)
                mmg(gb_, [
                    (lambda e, k=k, slg=slg, gb_=gb_: e.matmul(
                        PS[gb_][:, 0:GP], stMg.ap(slg, k), SB["g_n2b"][:, k, :],
                        start=k == 0, stop=k == 7))
                    for k in range(8)], [stMg.b(slg)] + n2bB)
                op("scalar", lambda e, mq=mq, gb_=gb_: e.activation(
                    out=SB["g_gg"][:, mq, :], in_=PS[gb_][:, 0:GP], func=AF.Gelu),
                    reads=[BF[gb_]], writes=[ggB[mq]])
                mmg(ub_, [
                    (lambda e, k=k, slu=slu, ub_=ub_: e.matmul(
                        PS[ub_][:, 0:GP], stMu.ap(slu, k), SB["g_n2b"][:, k, :],
                        start=k == 0, stop=k == 7))
                    for k in range(8)], [stMu.b(slu)] + n2bB)
                op("vector", lambda e, m=m, mq=mq, ub_=ub_: e.tensor_mul(
                    SB["g_pm"][:, m, :], SB["g_gg"][:, mq, :], PS[ub_][:, 0:GP]),
                    reads=[ggB[mq], BF[ub_]], writes=[pmB[m]])
            for ft in range(NFT):
                fq = ft % 2
                dnb = "pa" if fq == 0 else "pb"
                sld = stMd.load("mdownT", ft * P)
                mmg(dnb, [
                    (lambda e, k=k, sld=sld, dnb=dnb: e.matmul(
                        PS[dnb][:, :], stMd.ap(sld, k), SB["g_pm"][:, k, :],
                        start=k == 0, stop=k == NHT - 1))
                    for k in range(NHT)], [stMd.b(sld)] + pmB)
                op("vector", lambda e, ft=ft, dnb=dnb: e.tensor_add(
                    SB["g_xg"][:, ft, :], SB["g_xg"][:, ft, :], PS[dnb][:, :]),
                    reads=[xgB[ft], BF[dnb]], writes=[xgB[ft]])
            if pp == 0:
                self.tap("g_A", SB["g_xg"][:, :, :], xgB)
            for ft in range(NFT):
                fq = ft % 2
                op("vector", lambda e, ft=ft, fq=fq: e.tensor_copy(
                    SB["g_ab"][:, fq, :], SB["g_xg"][:, ft, :]), reads=[xgB[ft]], writes=[abB[fq]])
                op("scalar", lambda e, ft=ft, fq=fq: e.activation(
                    out=SB["g_sq"][:, fq, :], in_=SB["g_xg"][:, ft, :], func=AF.Square),
                    reads=[xgB[ft]], writes=[sqB[fq]])
                mmg("pc", [
                    (lambda e, ft=ft, fq=fq: e.matmul(
                        PS["pc"][0:1, :], SB["onesk"][:, :1], SB["g_ab"][:, fq, :],
                        start=ft == 0, stop=ft == NFT - 1))], [abB[fq], BF["onesk"]])
                mmg("pb", [
                    (lambda e, ft=ft, fq=fq: e.matmul(
                        PS["pb"][0:1, :], SB["onesk"][:, :1], SB["g_sq"][:, fq, :],
                        start=ft == 0, stop=ft == NFT - 1))], [sqB[fq], BF["onesk"]])
            op("scalar", lambda e: e.activation(
                out=SB["g_row1"][:1, :], in_=PS["pc"][0:1, :], func=AF.Copy, scale=1.0 / 1024),
                reads=[BF["pc"]], writes=[g3rowB])
            op("scalar", lambda e: e.activation(
                out=SB["g_row2"][:1, :], in_=PS["pb"][0:1, :], func=AF.Copy, scale=1.0 / 1024),
                reads=[BF["pb"]], writes=[g3rowB])
            op("vector", lambda e: e.tensor_tensor(
                SB["g_row3"][:1, :], SB["g_row1"][:1, :], SB["g_row1"][:1, :], op=ALU.mult),
                reads=[g3rowB], writes=[g3rowB])
            op("vector", lambda e: e.tensor_sub(
                SB["g_row2"][:1, :], SB["g_row2"][:1, :], SB["g_row3"][:1, :]),
                reads=[g3rowB], writes=[g3rowB])
            op("scalar", lambda e: e.activation(
                out=SB["g_row2"][:1, :], in_=SB["g_row2"][:1, :], func=AF.Sqrt,
                bias=SB["epsc"][0:1, 0:1]), reads=[g3rowB, BF["epsc"]], writes=[g3rowB])
            op("vector", lambda e: e.reciprocal(
                SB["g_row2"][:1, :], SB["g_row2"][:1, :]), reads=[g3rowB], writes=[g3rowB])
            bcast("ps0", SB["g_row1"][:1, 0:GP], g3rowB, GP)
            bcast("ps1", SB["g_row2"][:1, 0:GP], g3rowB, GP)
            for ft in range(NFT):
                op("vector", lambda e, ft=ft: e.tensor_sub(
                    SB["g_xg"][:, ft, :], SB["g_xg"][:, ft, :], PS["ps0"][:, 0:GP]),
                    reads=[xgB[ft], BF["ps0"]], writes=[xgB[ft]])
                op("vector", lambda e, ft=ft: e.tensor_mul(
                    SB["g_xg"][:, ft, :], SB["g_xg"][:, ft, :], PS["ps1"][:, 0:GP]),
                    reads=[xgB[ft], BF["ps1"]], writes=[xgB[ft]])
                op("vector", lambda e, ft=ft: e.tensor_add(
                    SB["g_xg"][:, ft, :], SB["g_xg"][:, ft, :], SB["g_xp"][:, ft, :]),
                    reads=[xgB[ft], xpB[ft]], writes=[xgB[ft]])
                dma("sync", DT["x1scr"][ts(ft), p0:p0 + GP], SB["g_xg"][:, ft, :],
                    reads=[xgB[ft]])
            if pp == NGP - 1:
                self.tap("x1", SB["g_xg"][:, :, :], xgB)

        self.barrier()
        self.close_scope(g3)
        self.close_scope(gall)

    def phaseF(self):
        SB, PS, BF, DT = self.SB, self.PS, self.BF, self.DT
        op, mmg, dma, bcast = self.op, self.mmg, self.dma, self.bcast
        fsc = self.scope()
        HB = self.sb("f_H", (P, NFT, 1 + T), F32, nbufs=NFT)
        fhbB = self.sb("f_hb", (P, 2, 512), BF16, nbufs=2)
        fsqB = self.sb("f_sq", (P, 2, 512), BF16, nbufs=2)
        self.sb("f_row1", (1, 512)); self.sb("f_row2", (1, 512)); self.sb("f_row3", (1, 512))
        self.sb("f_rb", (1, 512), BF16)
        frB = BF["f_row1"]
        for n in ("f_row2", "f_row3", "f_rb"):
            BF[n] = frB

        def ln_stats_512(get_bf16, get_sq, n_in, mu_slot, r_slot, extra=1.0):
            for i in range(n_in):
                ai, bi, ri = get_bf16(i)
                mmg("pc", [(lambda e, ai=ai, i=i: e.matmul(
                    PS["pc"][0:1, :], SB["onesk"][:, :1], ai,
                    start=i == 0, stop=i == n_in - 1))], [ri, BF["onesk"]])
                sai, sri = get_sq(i)
                mmg("pb", [(lambda e, sai=sai, i=i: e.matmul(
                    PS["pb"][0:1, :], SB["onesk"][:, :1], sai,
                    start=i == 0, stop=i == n_in - 1))], [sri, BF["onesk"]])
            op("scalar", lambda e: e.activation(
                out=SB["f_row1"][:1, :], in_=PS["pc"][0:1, :], func=AF.Copy, scale=1.0 / 1024),
                reads=[BF["pc"]], writes=[frB])
            op("scalar", lambda e: e.activation(
                out=SB["f_row2"][:1, :], in_=PS["pb"][0:1, :], func=AF.Copy, scale=1.0 / 1024),
                reads=[BF["pb"]], writes=[frB])
            op("vector", lambda e: e.tensor_tensor(
                SB["f_row3"][:1, :], SB["f_row1"][:1, :], SB["f_row1"][:1, :], op=ALU.mult),
                reads=[frB], writes=[frB])
            op("vector", lambda e: e.tensor_sub(
                SB["f_row2"][:1, :], SB["f_row2"][:1, :], SB["f_row3"][:1, :]),
                reads=[frB], writes=[frB])
            op("scalar", lambda e: e.activation(
                out=SB["f_row2"][:1, :], in_=SB["f_row2"][:1, :], func=AF.Sqrt, bias=SB["epsc"][0:1, 0:1]),
                reads=[frB, BF["epsc"]], writes=[frB])
            op("vector", lambda e: e.reciprocal(
                SB["f_row2"][:1, :], SB["f_row2"][:1, :]), reads=[frB], writes=[frB])
            bcast("pa", SB["f_row1"][:1, 0:512], frB, 512)
            bcast("pb", SB["f_row2"][:1, 0:512], frB, 512)

        for fc in range(NFC):
            o0 = fc * FC
            for ft in range(NFT):
                dma("sync", SB["f_H"][:, ft, 1 + o0:1 + o0 + FC],
                    DT["x1scr"][ts(ft), o0:o0 + FC], writes=[HB[ft]])
            for o, w in npieces(FC):
                oo = o0 + o

                def getb(i, oo=oo):
                    ft = i
                    sl_ = i % 2
                    op("vector", lambda e, ft=ft, oo=oo, sl_=sl_: e.tensor_copy(
                        SB["f_hb"][:, sl_, :], SB["f_H"][:, ft, 1 + oo:1 + oo + 512]),
                        reads=[HB[ft]], writes=[fhbB[sl_]])
                    return SB["f_hb"][:, sl_, :], None, fhbB[sl_]

                def gets(i, oo=oo):
                    ft = i
                    sl_ = i % 2
                    op("scalar", lambda e, ft=ft, oo=oo, sl_=sl_: e.activation(
                        out=SB["f_sq"][:, sl_, :], in_=SB["f_H"][:, ft, 1 + oo:1 + oo + 512],
                        func=AF.Square), reads=[HB[ft]], writes=[fsqB[sl_]])
                    return SB["f_sq"][:, sl_, :], fsqB[sl_]
                ln_stats_512(getb, gets, NFT, 0, 1)
                for ft in range(NFT):
                    op("vector", lambda e, ft=ft, oo=oo: e.tensor_sub(
                        SB["f_H"][:, ft, 1 + oo:1 + oo + 512],
                        SB["f_H"][:, ft, 1 + oo:1 + oo + 512], PS["pa"][:, :]),
                        reads=[HB[ft], BF["pa"]], writes=[HB[ft]])
                    op("vector", lambda e, ft=ft, oo=oo: e.tensor_mul(
                        SB["f_H"][:, ft, 1 + oo:1 + oo + 512],
                        SB["f_H"][:, ft, 1 + oo:1 + oo + 512], PS["pb"][:, :]),
                        reads=[HB[ft], BF["pb"]], writes=[HB[ft]])
        for ft in range(NFT):
            op("vector", lambda e, ft=ft: e.memset(SB["f_H"][:, ft, 0:1], 0.0),
                reads=[], writes=[HB[ft]])
        self.tap("f_h1", SB["f_H"][:, :, :], HB)

        wkqB = self.sb("f_wkq", (P, 2, 512), F32, nbufs=2)
        wkq2B = self.sb("f_wkq2", (P, 2, 512), F32, nbufs=2)
        self.sb("f_hct", (P, NFT))
        hctB = BF["f_hct"]

        def emit_layer(l):
            SB = dict(self.SB)
            asc = self.scope()
            self.sb(f"f_pad{l}", (P, 16 + 16 * l), BF16)
            kB = self.sb("f_k", (P, 4, 2560), BF16, nbufs=4)
            vB = self.sb("f_v", (P, 20, 512), BF16, nbufs=20)
            qB = self.sb("f_q", (P, 4, T), BF16, nbufs=4)
            oB = self.sb("f_ow", (P, 4, 512), BF16, nbufs=4)
            maskB = self.sb("f_mask", (P, 2, 4, 640), BF16, nbufs=2)
            stWoS = Prog.WStrip(self, "sWoS", 4, P)
            qsc = self.scope()
            self.sb("f_qraw", (P, 2, 512), F32, nbufs=2)
            qrawB = BF["f_qraw"]
            self.sb("f_qn", (P, 2, 512), BF16, nbufs=2)
            qnB = BF["f_qn"]
            self.sb("f_hbw", (P, NFT, 513), BF16, nbufs=NFT)
            hbwB = BF["f_hbw"]
            self.sb("f_wv", (P, 8, 512), BF16)
            wvB = BF["f_wv"]
            self.sb("f_tab", (P, 2, 512), F32, nbufs=2)
            tabB = BF["f_tab"]
            self.sb("f_nr", (2, 2, 512), F32, nbufs=2)
            nrB = BF["f_nr"]
            self.sb("f_nrb", (2, 2, 512), BF16, nbufs=2)
            nrbB = BF["f_nrb"]
            stQK = Prog.WStrip(self, "sQK", 8, P)
            SB = dict(self.SB)
            for kk in range(8):
                dma("gpsimd", SB["f_wv"][:, kk, :],
                    DT[f"wqkv{l}"][ts(kk), 1024:1536], writes=[wvB])

            for w in (3, 1, 2, 0):
                wbase = w * WIN
                for ft in range(NFT):
                    op("vector", lambda e, ft=ft, wbase=wbase: e.tensor_copy(
                        SB["f_hbw"][:, ft, :], SB["f_H"][:, ft, wbase:wbase + 513]),
                        reads=[HB[ft]], writes=[hbwB[ft]])
                def ldtab(isq, wbase=wbase):
                    tC = "tabCq" if isq else "tabCk"
                    tS = "tabSq" if isq else "tabSk"
                    sc1 = f"qs{l}" if isq else f"ks{l}"
                    sc2 = f"qssw{l}" if isq else f"kssw{l}"
                    dma("sync", SB["f_tab"][:, 0, :], DT[tC][:, wbase:wbase + 512],
                        writes=[tabB[0]])
                    dma("sync", SB["f_tab"][:, 1, :], DT[tS][:, wbase:wbase + 512],
                        writes=[tabB[1]])
                    op("vector", lambda e, sc1=sc1: e.tensor_scalar(
                        SB["f_tab"][:, 0, :], SB["f_tab"][:, 0, :], SB[sc1][:, 0:1],
                        None, op0=ALU.mult), reads=[tabB[0], BF[sc1]], writes=[tabB[0]])
                    op("vector", lambda e, sc2=sc2: e.tensor_scalar(
                        SB["f_tab"][:, 1, :], SB["f_tab"][:, 1, :], SB[sc2][:, 0:1],
                        None, op0=ALU.mult), reads=[tabB[1], BF[sc2]], writes=[tabB[1]])

                def qA(m):
                    sl = stQK.load(f"wqkv{l}", m * P)
                    psn = "ps0" if m % 2 == 0 else "ps1"
                    mmg(psn, [
                        (lambda e, k=k, sl=sl, psn=psn: e.matmul(
                            PS[psn][:, 0:512], stQK.ap(sl, k),
                            SB["f_hbw"][:, k, 1:513] if k < 4 else SB["f_hbw"][:, k, 0:512],
                            start=k == 0, stop=k == 7))
                        for k in range(8)], [stQK.b(sl)] + hbwB)

                def qB_(m):
                    psn = "ps0" if m % 2 == 0 else "ps1"
                    mp = m % 2
                    nbk = "pb" if m % 2 == 0 else "pc"
                    op("scalar", lambda e, psn=psn, mp=mp: e.activation(
                        out=SB["f_qraw"][:, mp, :], in_=PS[psn][:, 0:512], func=AF.Copy),
                        reads=[BF[psn]], writes=[qrawB[mp]])
                    op("scalar", lambda e, mp=mp: e.activation(
                        out=SB["f_sq"][:, mp, :], in_=SB["f_qraw"][:, mp, :], func=AF.Square),
                        reads=[qrawB[mp]], writes=[fsqB[mp]])
                    mmg(nbk, [(lambda e, nbk=nbk, mp=mp: e.matmul(
                        PS[nbk][0:2, :], SB["blockones"][:, :], SB["f_sq"][:, mp, :],
                        start=True, stop=True))], [fsqB[mp], BF["blockones"]])
                    op("scalar", lambda e, mp=mp, nbk=nbk: e.activation(
                        out=SB["f_nr"][:2, mp, :], in_=PS[nbk][0:2, :], func=AF.Sqrt),
                        reads=[BF[nbk]], writes=[nrB[mp]])
                    op("vector", lambda e, mp=mp: e.tensor_scalar_max(
                        SB["f_nr"][:2, mp, :], SB["f_nr"][:2, mp, :], 1e-12),
                        reads=[nrB[mp]], writes=[nrB[mp]])
                    op("vector", lambda e, mp=mp: e.reciprocal(
                        SB["f_nr"][:2, mp, :], SB["f_nr"][:2, mp, :]),
                        reads=[nrB[mp]], writes=[nrB[mp]])
                    op("vector", lambda e, mp=mp: e.tensor_copy(
                        SB["f_nrb"][:2, mp, :], SB["f_nr"][:2, mp, :]),
                        reads=[nrB[mp]], writes=[nrbB[mp]])

                def qC(m):
                    mp = m % 2
                    nbk = "pb" if m % 2 == 0 else "pc"
                    mmg(nbk, [(lambda e, mp=mp, nbk=nbk: e.matmul(
                        PS[nbk][:, :], SB["pat2"][:2, :], SB["f_nrb"][:2, mp, :],
                        start=True, stop=True))], [nrbB[mp], BF["pat2"]])
                    op("vector", lambda e, mp=mp, nbk=nbk: e.tensor_mul(
                        SB["f_qn"][:, mp, :], SB["f_qraw"][:, mp, :], PS[nbk][:, :]),
                        reads=[qrawB[mp], BF[nbk]], writes=[qnB[mp]])
                    mmg(nbk, [(lambda e, mp=mp, nbk=nbk: e.matmul(
                        PS[nbk][:, :], SB["perm"][:, :], SB["f_qn"][:, mp, :],
                        start=True, stop=True))], [qnB[mp], BF["perm"]])

                def qD(m, wbase=wbase):
                    mp = m % 2
                    nbk = "pb" if m % 2 == 0 else "pc"
                    isq = m < 4
                    tp = m % 4
                    op("vector", lambda e, mp=mp: e.tensor_mul(
                        SB["f_wkq"][:, mp, :], SB["f_qn"][:, mp, :], SB["f_tab"][:, 0, :]),
                        reads=[qnB[mp], tabB[0]], writes=[wkqB[mp]])
                    op("vector", lambda e, mp=mp, nbk=nbk: e.tensor_mul(
                        SB["f_wkq2"][:, mp, :], PS[nbk][:, :], SB["f_tab"][:, 1, :]),
                        reads=[BF[nbk], tabB[1]], writes=[wkq2B[mp]])
                    if isq:
                        op("vector", lambda e, tp=tp, mp=mp, wbase=wbase: e.tensor_add(
                            SB["f_q"][:, tp, wbase:wbase + 512], SB["f_wkq"][:, mp, :],
                            SB["f_wkq2"][:, mp, :]),
                            reads=[wkqB[mp], wkq2B[mp]], writes=[qB[tp]])
                    else:
                        op("vector", lambda e, tp=tp, mp=mp, wbase=wbase: e.tensor_add(
                            SB["f_k"][:, tp, 512 + wbase:1024 + wbase],
                            SB["f_wkq"][:, mp, :], SB["f_wkq2"][:, mp, :]),
                            reads=[wkqB[mp], wkq2B[mp]], writes=[kB[tp]])

                ldtab(True)
                for step in range(4 + 3):
                    if 3 <= step:
                        qD(step - 3)
                    if 2 <= step and step - 2 < 4:
                        qC(step - 2)
                    if 1 <= step and step - 1 < 4:
                        qB_(step - 1)
                    if step < 4:
                        qA(step)
                ldtab(False)
                for step in range(4 + 3):
                    if 3 <= step:
                        qD(4 + step - 3)
                    if 2 <= step and step - 2 < 4:
                        qC(4 + step - 2)
                    if 1 <= step and step - 1 < 4:
                        qB_(4 + step - 1)
                    if step < 4:
                        qA(4 + step)
                for tt in range(4):
                    vt = 4 * w + tt + 4
                    mmg("pa", [
                        (lambda e, kk=kk, tt=tt: e.matmul(
                            PS["pa"][:, :], SB["f_hbw"][:, kk, 1 + tt * P:1 + tt * P + P]
                            if kk < 4 else SB["f_hbw"][:, kk, tt * P:tt * P + P],
                            SB["f_wv"][:, kk, :], start=kk == 0, stop=kk == 7))
                        for kk in range(8)], hbwB + [wvB])
                    op("scalar", lambda e, vt=vt: e.activation(
                        out=SB["f_v"][:, vt, :], in_=PS["pa"][:, :], func=AF.Copy),
                        reads=[BF["pa"]], writes=[vB[vt]])
                if w == 3:
                    kvB = self.buf(f"kv{l}")
                    for i in range(4):
                        dma("sync", DT[f"cin_kv{l}"][i, :, :], SB["f_k"][:, i, 2048:2560],
                            reads=[kB[i]], writes=[kvB])
                    for i in range(4):
                        dma("sync", DT[f"cin_kv{l}"][4 + i, :, :], SB["f_v"][:, 16 + i, :],
                            reads=[vB[16 + i]], writes=[kvB])
                    hcB = self.buf(f"hc{l}")
                    for ft in range(NFT):
                        dma("sync", DT[f"cin_hc{l}"][ft:ft + 1, :].rearrange("o p -> p o"),
                            SB["f_H"][:, ft, T:T + 1], reads=[HB[ft]], writes=[hcB])
                    op("gpsimd", lambda e, l=l: e.collective_compute(
                        "AllGather", ALU.bypass, replica_groups=RG,
                        ins=[DT[f"cin_kv{l}"].opt()], outs=[DT[f"cout_kv{l}"].opt()]),
                        reads=[kvB], writes=[kvB])
                    op("gpsimd", lambda e, l=l: e.collective_compute(
                        "AllGather", ALU.bypass, replica_groups=RG,
                        ins=[DT[f"cin_hc{l}"].opt()], outs=[DT[f"cout_hc{l}"].opt()]),
                        reads=[hcB], writes=[hcB])
                    for i in range(4):
                        dma("sync", SB["f_k"][:, i, 0:512], DT[f"cout_kv{l}"][0, i, :, :],
                            reads=[kvB], writes=[kB[i]])
                    for i in range(4):
                        dma("sync", SB["f_v"][:, i, :], DT[f"cout_kv{l}"][0, 4 + i, :, :],
                            reads=[kvB], writes=[vB[i]])
                    for ft in range(NFT):
                        dma("sync", SB["f_hct"][:, ft:ft + 1],
                            DT[f"cout_hc{l}"][0, ft:ft + 1, :].rearrange("o p -> p o"),
                            reads=[hcB], writes=[hctB])
                    op("vector", lambda e: e.tensor_scalar(
                        SB["f_hct"][:, :], SB["f_hct"][:, :], SB["selmask"][:, 0:1],
                        None, op0=ALU.mult), reads=[hctB, BF["selmask"]], writes=[hctB])
                    for ft in range(NFT):
                        op("vector", lambda e, ft=ft: e.tensor_copy(
                            SB["f_H"][:, ft, 0:1], SB["f_hct"][:, ft:ft + 1]),
                            reads=[hctB], writes=[HB[ft]])
                else:
                    pass
            if l == 0:
                self.tap("f_q", SB["f_q"][:, :, :], qB)
                self.tap("f_k", SB["f_k"][:, :, :], kB)
                self.tap("f_v", SB["f_v"][:, :, :], vB)

            self.barrier()
            self.close_scope(qsc)
            self.sb("f_p", (P, 4, 640), BF16, nbufs=4)
            pB = BF["f_p"]
            self.sb("f_rect", (P, 4, 8, 512), BF16, nbufs=4)
            rectB = BF["f_rect"]
            self.sb("f_ell", (P, 4), F32, nbufs=4)
            ellB = BF["f_ell"]
            self.sb("f_negm", (P, 4), F32, nbufs=4)
            negmB = BF["f_negm"]
            SB = dict(self.SB)
            for rr in range(4):
                op("vector", lambda e, rr=rr: e.memset(SB["f_rect"][:, rr, :, :], 0.0),
                    writes=[rectB[rr]])

            for w in range(NW):
                mname = "maskA" if w == 0 else "maskB"
                wsl = w % 2
                dma("gpsimd", SB["f_mask"][:, wsl, :, :],
                    DT[mname].rearrange("q p m -> p q m"), writes=[maskB[wsl]])
                if True:
                    items = [(tp, qc, hh) for tp in range(4) for qc in range(4)
                             for hh in range(2)]

                    def stA(i, w=w):
                        tp, qc, hh = items[i]
                        q0 = qc * P
                        band0 = w * WIN + qc * P
                        psn = "ps0" if hh == 0 else "ps1"
                        hof = 64 * hh
                        mmg(psn, [
                            (lambda e, o=o, wd=wd, hof=hof, tp=tp, q0=q0, band0=band0,
                             psn=psn, w=w: e.matmul(
                                PS[psn][:, o:o + wd],
                                SB["f_q"][hof:hof + 64, tp, w * WIN + q0:w * WIN + q0 + P],
                                SB["f_k"][hof:hof + 64, tp, band0 + o:band0 + o + wd],
                                start=True, stop=True,
                                tile_position=(hof, 0)))
                            for o, wd in npieces(640)],
                            [qB[tp], kB[tp]])

                    def stB(i, w=w):
                        tp, qc, hh = items[i]
                        psn = "ps0" if hh == 0 else "ps1"
                        psl = i % 4
                        wsl = w % 2
                        op("vector", lambda e, psn=psn, qc=qc, wsl=wsl: e.tensor_tensor(
                            PS[psn][:, :], PS[psn][:, :], SB["f_mask"][:, wsl, qc, :],
                            op=ALU.add), reads=[BF[psn], maskB[wsl]], writes=[BF[psn]])
                        op("vector", lambda e, psn=psn, psl=psl: e.tensor_reduce(
                            SB["f_negm"][:, psl:psl + 1], PS[psn][:, :], axis=AX.X,
                            op=ALU.max, negate=True), reads=[BF[psn]], writes=[negmB[psl]])

                    def stC(i):
                        tp, qc, hh = items[i]
                        psn = "ps0" if hh == 0 else "ps1"
                        psl = i % 4
                        op("scalar", lambda e, psn=psn, psl=psl: e.activation(
                            out=SB["f_p"][:, psl, :], in_=PS[psn][:, :], func=AF.Exp,
                            bias=SB["f_negm"][:, psl:psl + 1],
                            accum_out=SB["f_ell"][:, psl:psl + 1]),
                            reads=[BF[psn], negmB[psl]], writes=[pB[psl], ellB[psl]])
                        op("vector", lambda e, psl=psl: e.reciprocal(
                            SB["f_ell"][:, psl:psl + 1], SB["f_ell"][:, psl:psl + 1]),
                            reads=[ellB[psl]], writes=[ellB[psl]])

                    def stD(i):
                        psl = i % 4
                        op("scalar", lambda e, psl=psl: e.activation(
                            out=SB["f_p"][:, psl, :], in_=SB["f_p"][:, psl, :],
                            func=AF.Copy, scale=SB["f_ell"][:, psl:psl + 1]),
                            reads=[pB[psl], ellB[psl]], writes=[pB[psl]])

                    def stE(i):
                        tp, qc, hh = items[i]
                        rpar = 2 * (tp % 2)
                        psl = i % 4
                        rsl = hh + rpar
                        q0 = qc * P
                        for b in range(4):
                            op("tensor", lambda e, psl=psl, b=b: e.transpose(
                                PS["pt"][:, b * P:(b + 1) * P],
                                SB["f_p"][:, psl, b * P:(b + 1) * P],
                                SB["ident"][:, :]),
                                reads=[pB[psl], BF["ident"]], writes=[BF[f"pt{b}"]])
                        op("scalar", lambda e, rsl=rsl, qc=qc, q0=q0: e.activation(
                            out=SB["f_rect"][:, rsl, qc:qc + 4, q0:q0 + P],
                            in_=PS["pt"][:, 0:512].rearrange("p (b q) -> p b q", b=4),
                            func=AF.Copy),
                            reads=[BF["pt0"], BF["pt1"], BF["pt2"], BF["pt3"]],
                            writes=[rectB[rsl]])
                        op("tensor", lambda e, psl=psl: e.transpose(
                            PS["pt"][:, 0:P], SB["f_p"][:, psl, 4 * P:5 * P],
                            SB["ident"][:, :]),
                            reads=[pB[psl], BF["ident"]], writes=[BF["pt0"]])
                        op("scalar", lambda e, rsl=rsl, qc=qc, q0=q0: e.activation(
                            out=SB["f_rect"][:, rsl, qc + 4, q0:q0 + P],
                            in_=PS["pt"][:, 0:P], func=AF.Copy),
                            reads=[BF["pt0"]], writes=[rectB[rsl]])

                    def stPV(tp, w=w):
                        rpar = 2 * (tp % 2)
                        pvb = "pa" if tp % 2 == 0 else "pc"
                        mmg(pvb, [
                            (lambda e, kc=kc, hh=hh, w=w, tp=tp, pvb=pvb: e.matmul(
                                PS[pvb][64 * hh:64 * hh + 64, :],
                                SB["f_v"][:, 4 * w + kc,
                                          64 * (2 * tp + hh):64 * (2 * tp + hh) + 64],
                                SB["f_rect"][:, hh + 2 * (tp % 2), kc, :],
                                start=kc == 0, stop=kc == 7, tile_position=(0, 64 * hh)))
                            for hh in range(2) for kc in range(8)],
                            [vB[4 * w + kc2] for kc2 in range(8)] +
                            [rectB[rpar], rectB[rpar + 1]])
                        op("scalar", lambda e, tp=tp, pvb=pvb: e.activation(
                            out=SB["f_ow"][:, tp, :], in_=PS[pvb][:, :], func=AF.Copy),
                            reads=[BF[pvb]], writes=[oB[tp]])

                    NI = len(items)
                    for step in range(NI + 4):
                        if 4 <= step and step - 4 < NI:
                            stE(step - 4)
                            if (step - 4) % 8 == 7:
                                stPV((step - 4) // 8)
                        if 3 <= step and step - 3 < NI:
                            stD(step - 3)
                        if 2 <= step and step - 2 < NI:
                            stC(step - 2)
                        if 1 <= step and step - 1 < NI:
                            stB(step - 1)
                        if step < NI:
                            stA(step)
                for ft in range(NFT):
                    slw = stWoS.load(f"wo{l}", ft * P)
                    mmg("pb", [
                        (lambda e, kt=kt, slw=slw: e.matmul(
                            PS["pb"][:, :], stWoS.ap(slw, kt), SB["f_ow"][:, kt, :],
                            start=kt == 0, stop=kt == 3))
                        for kt in range(4)], [stWoS.b(slw)] + oB)
                    op("vector", lambda e, ft=ft, w=w: e.tensor_add(
                        SB["f_H"][:, ft, 1 + w * WIN:1 + w * WIN + 512],
                        SB["f_H"][:, ft, 1 + w * WIN:1 + w * WIN + 512], PS["pb"][:, :]),
                        reads=[HB[ft], BF["pb"]], writes=[HB[ft]])
            if l == 0:
                self.tap("f_o", SB["f_ow"][:, :, :], oB)
                self.tap("f_pa", SB["f_H"][:, :, :], HB)

            paB = self.buf(f"pa{l}")
            for ft in range(NFT):
                dma("sync", DT[f"cin_pa{l}"][ft:ft + 1, :].rearrange("o p -> p o"),
                    SB["f_H"][:, ft, T:T + 1], reads=[HB[ft]], writes=[paB])
            op("gpsimd", lambda e, l=l: e.collective_compute(
                "AllGather", ALU.bypass, replica_groups=RG,
                ins=[DT[f"cin_pa{l}"].opt()], outs=[DT[f"cout_pa{l}"].opt()]),
                reads=[paB], writes=[paB])
            for ft in range(NFT):
                dma("sync", SB["f_hct"][:, ft:ft + 1],
                    DT[f"cout_pa{l}"][0, ft:ft + 1, :].rearrange("o p -> p o"),
                    reads=[paB], writes=[hctB])
            op("vector", lambda e: e.tensor_scalar(
                SB["f_hct"][:, :], SB["f_hct"][:, :], SB["selmask"][:, 0:1],
                None, op0=ALU.mult), reads=[hctB, BF["selmask"]], writes=[hctB])
            for ft in range(NFT):
                op("vector", lambda e, ft=ft: e.tensor_copy(
                    SB["f_H"][:, ft, 0:1], SB["f_hct"][:, ft:ft + 1]),
                    reads=[hctB], writes=[HB[ft]])

            self.barrier()
            self.close_scope(asc)
            fsc2 = self.scope()
            self.sb(f"f_pad2{l}", (P, 48 + 16 * l), BF16)
            hnB = self.sb("f_hn", (P, NFT, T), BF16, nbufs=NFT)
            prodB = self.sb("f_prod", (P, MT1P, FC), BF16, nbufs=MT1P)
            fggB = self.sb("f_gg", (P, 2, 512), F32, nbufs=2)
            stW1 = Prog.WStrip(self, "sW1", 8, 256)
            stW2 = Prog.WStrip(self, "sW2", KT2, P)
            SB = dict(self.SB)

            for fc in (1, 0):
                o0 = fc * FC
                for o, wdt in npieces(FC):
                    oo = o0 + o

                    def getb(i, oo=oo):
                        ft = i
                        sl_ = i % 2
                        sh = 1 if ft < 4 else 0
                        op("vector", lambda e, ft=ft, oo=oo, sh=sh, sl_=sl_: e.tensor_copy(
                            SB["f_hb"][:, sl_, :], SB["f_H"][:, ft, sh + oo:sh + oo + 512]),
                            reads=[HB[ft]], writes=[fhbB[sl_]])
                        return SB["f_hb"][:, sl_, :], None, fhbB[sl_]

                    def gets(i, oo=oo):
                        ft = i
                        sl_ = i % 2
                        sh = 1 if ft < 4 else 0
                        op("scalar", lambda e, ft=ft, oo=oo, sh=sh, sl_=sl_: e.activation(
                            out=SB["f_sq"][:, sl_, :], in_=SB["f_H"][:, ft, sh + oo:sh + oo + 512],
                            func=AF.Square), reads=[HB[ft]], writes=[fsqB[sl_]])
                        return SB["f_sq"][:, sl_, :], fsqB[sl_]
                    ln_stats_512(getb, gets, NFT, 0, 1)
                    for ft in range(NFT):
                        sh = 1 if ft < 4 else 0
                        wsl = ft % 2
                        op("vector", lambda e, ft=ft, oo=oo, sh=sh, wsl=wsl: e.tensor_sub(
                            SB["f_wkq"][:, wsl, :], SB["f_H"][:, ft, sh + oo:sh + oo + 512],
                            PS["pa"][:, :]), reads=[HB[ft], BF["pa"]], writes=[wkqB[wsl]])
                        op("vector", lambda e, wsl=wsl: e.tensor_mul(
                            SB["f_wkq"][:, wsl, :], SB["f_wkq"][:, wsl, :], PS["pb"][:, :]),
                            reads=[wkqB[wsl], BF["pb"]], writes=[wkqB[wsl]])
                        op("vector", lambda e, ft=ft, oo=oo, wsl=wsl: e.tensor_scalar(
                            SB["f_hn"][:, ft, oo:oo + 512], SB["f_wkq"][:, wsl, :],
                            SB[f"lng{l}"][:, ft:ft + 1], SB[f"lnb{l}"][:, ft:ft + 1],
                            op0=ALU.mult, op1=ALU.add),
                            reads=[wkqB[wsl], BF[f"lng{l}"], BF[f"lnb{l}"]],
                            writes=[hnB[ft]])
            for fc in (1, 0):
                o0 = fc * FC
                for j in range(MT1P):
                    jq = j % 2
                    ab_ = "pa" if jq == 0 else "pc"
                    gb2 = "pb" if jq == 0 else "ps0"
                    sl = stW1.load(f"w1p{l}", j * 256)
                    for o, wdt in npieces(FC):
                        mmg(ab_, [
                            (lambda e, k=k, sl=sl, o=o, o0=o0, ab_=ab_: e.matmul(
                                PS[ab_][:, 0:512], stW1.ap(sl, k, slice(0, 128)),
                                SB["f_hn"][:, k, o0 + o:o0 + o + 512], start=k == 0, stop=k == 7))
                            for k in range(8)], [stW1.b(sl)] + hnB)
                        mmg(gb2, [
                            (lambda e, k=k, sl=sl, o=o, o0=o0, gb2=gb2: e.matmul(
                                PS[gb2][:, 0:512], stW1.ap(sl, k, slice(128, 256)),
                                SB["f_hn"][:, k, o0 + o:o0 + o + 512], start=k == 0, stop=k == 7))
                            for k in range(8)], [stW1.b(sl)] + hnB)
                        op("scalar", lambda e, j=j, jq=jq, gb2=gb2: e.activation(
                            out=SB["f_gg"][:, jq, :], in_=PS[gb2][:, 0:512], func=AF.Gelu,
                            bias=SB[f"b1p{l}"][:, 2 * j + 1:2 * j + 2]),
                            reads=[BF[gb2], BF[f"b1p{l}"]], writes=[fggB[jq]])
                        op("vector", lambda e, j=j, jq=jq, o=o, ab_=ab_: e.scalar_tensor_tensor(
                            SB["f_prod"][:, j, o:o + 512], PS[ab_][:, 0:512],
                            SB[f"b1p{l}"][:, 2 * j:2 * j + 1], SB["f_gg"][:, jq, :],
                            op0=ALU.add, op1=ALU.mult),
                            reads=[BF[ab_], BF[f"b1p{l}"], fggB[jq]], writes=[prodB[j]])
                for ft in range(NFT):
                    fq = ft % 2
                    w2b = "pa" if fq == 0 else "pb"
                    sl = stW2.load(f"w2p{l}", ft * P)
                    for o, wdt in npieces(FC):
                        mmg(w2b, [
                            (lambda e, k=k, sl=sl, o=o, w2b=w2b: e.matmul(
                                PS[w2b][:, :], stW2.ap(sl, k), SB["f_prod"][:, k, o:o + 512],
                                start=k == 0, stop=k == KT2 - 1))
                            for k in range(KT2)], [stW2.b(sl)] + prodB)
                        op("vector", lambda e, ft=ft, o0=o0, o=o, w2b=w2b: e.scalar_tensor_tensor(
                            SB["f_H"][:, ft, 1 + o0 + o:1 + o0 + o + 512], PS[w2b][:, :],
                            SB[f"b2{l}"][:, ft:ft + 1],
                            SB["f_H"][:, ft, 1 + o0 + o:1 + o0 + o + 512],
                            op0=ALU.add, op1=ALU.add),
                            reads=[BF[w2b], BF[f"b2{l}"], HB[ft]], writes=[HB[ft]])
            if l == 0:
                self.tap("f_out", SB["f_H"][:, :, :], HB)
            self.barrier()
            self.close_scope(fsc2)

        for l in range(2):
            emit_layer(l)

        x1B = self.sb("f_x1", (P, NFT, 512), F32, nbufs=NFT)
        outB = self.sb("f_yt", (P, 512), F32)
        for o in range(0, T, 512):
            def getb(i, o=o):
                ft = i
                sl_ = i % 2
                op("vector", lambda e, ft=ft, o=o, sl_=sl_: e.tensor_copy(
                    SB["f_hb"][:, sl_, :], SB["f_H"][:, ft, 1 + o:1 + o + 512]),
                    reads=[HB[ft]], writes=[fhbB[sl_]])
                return SB["f_hb"][:, sl_, :], None, fhbB[sl_]

            def gets(i, o=o):
                ft = i
                sl_ = i % 2
                op("scalar", lambda e, ft=ft, o=o, sl_=sl_: e.activation(
                    out=SB["f_sq"][:, sl_, :], in_=SB["f_H"][:, ft, 1 + o:1 + o + 512],
                    func=AF.Square), reads=[HB[ft]], writes=[fsqB[sl_]])
                return SB["f_sq"][:, sl_, :], fsqB[sl_]
            ln_stats_512(getb, gets, NFT, 0, 1)
            for ft in range(NFT):
                dma("sync", SB["f_x1"][:, ft, :], DT["x1scr"][ts(ft), o:o + 512],
                    writes=[x1B[ft]])
                op("vector", lambda e, ft=ft, o=o: e.tensor_sub(
                    SB["f_yt"][:, :], SB["f_H"][:, ft, 1 + o:1 + o + 512], PS["pa"][:, :]),
                    reads=[HB[ft], BF["pa"]], writes=[BF["f_yt"]])
                op("vector", lambda e: e.tensor_mul(
                    SB["f_yt"][:, :], SB["f_yt"][:, :], PS["pb"][:, :]),
                    reads=[BF["f_yt"], BF["pb"]], writes=[BF["f_yt"]])
                op("vector", lambda e, ft=ft: e.tensor_add(
                    SB["f_yt"][:, :], SB["f_yt"][:, :], SB["f_x1"][:, ft, :]),
                    reads=[BF["f_yt"], x1B[ft]], writes=[BF["f_yt"]])
                dma("sync", DT["y"][ts(ft), o:o + 512], SB["f_yt"][:, :],
                    reads=[BF["f_yt"]])
        self.barrier()
        self.close_scope(fsc)


def build_program(taps=(), upto=None):
    return Prog(taps, upto=upto).build()


def kernel(**inputs):
    shared, per_core = _prep_inputs(inputs)
    nc = build_program()
    in_maps = []
    for c in range(8):
        d = dict(shared)
        d.update(per_core[c])
        in_maps.append(d)
    res = run_bass_kernel_spmd(nc, in_maps, core_ids=list(range(8)))
    out = np.zeros((4, 4096, 1024), np.float32)
    for c in range(8):
        b, half = c // 2, c % 2
        out[b, half * T:(half + 1) * T, :] = res.results[c]["y"].T
    return out
